# version 04
# baseline (speedup 1.0000x reference)
import numpy as np
import ml_dtypes
from contextlib import ExitStack

import concourse.bass as bass
import concourse.mybir as mybir
from concourse.bass_utils import run_bass_kernel_spmd

F32 = mybir.dt.float32
BF16 = mybir.dt.bfloat16
I32 = mybir.dt.int32
AF = mybir.ActivationFunctionType
ALU = mybir.AluOpType
AX = mybir.AxisListType

NCO = 8
S = 8192
D = 4096
TPC = S // NCO
NTT = TPC // 128
G = D // 128
NH = 4
TC = S // 512
VT = S // 128
QB = 16
SCALE = float(1.0 / np.sqrt(128.0))
EXP_BIAS = -10.0
LN_EPS = 1e-5
RMS_EPS = 1.1920929e-07
SINK_ITERS = 50

_TAPS = ()
_SERIAL = False
_SERIAL_STAGE = None
_SERIAL_MASK = 'PADLG'

ENGS = ("sync", "gpsimd", "tensor", "vector", "scalar")


class Ctr:
    def __init__(self, es, nc, name):
        self.sem = es.enter_context(nc.semaphore(name))
        self.n = 0


class Prog:

    def __init__(self):
        self.q = {e: [] for e in ENGS}

    def on(self, eng, fn):
        self.q[eng].append(fn)

    def flush(self, blk):
        for eng in ENGS:
            fns = self.q[eng]
            getattr(blk, eng)(lambda e, fns=fns: [f(e) for f in fns])


def build_program():
    nc = bass.Bass("TRN2", target_bir_lowering=False)
    es = ExitStack()
    P = Prog()

    x_in = nc.dram_tensor("x_slice", [TPC, D], F32, kind="ExternalInput")
    wq_in = nc.dram_tensor("wq_s", [D, 512], F32, kind="ExternalInput")
    wk_in = nc.dram_tensor("wk_s", [D, 512], F32, kind="ExternalInput")
    wv_in = nc.dram_tensor("wv_s", [D, 512], F32, kind="ExternalInput")
    wo_in = nc.dram_tensor("wo", [D, D], F32, kind="ExternalInput")
    wbr_in = nc.dram_tensor("wbr", [1024, 1024], F32, kind="ExternalInput")
    wall_in = nc.dram_tensor("wall", [24, D], F32, kind="ExternalInput")
    bases_in = nc.dram_tensor("bases", [1, 24], F32, kind="ExternalInput")
    gate_in = nc.dram_tensor("gate", [1, 1], F32, kind="ExternalInput")
    eye_in = nc.dram_tensor("eye", [128, 128], BF16, kind="ExternalInput")
    ones_in = nc.dram_tensor("onesb", [128, 128], BF16, kind="ExternalInput")
    iota_d_in = nc.dram_tensor("iota_d", [1, 128], F32, kind="ExternalInput")
    iota_t_in = nc.dram_tensor("iota_t", [1, S], F32, kind="ExternalInput")
    cmask_in = nc.dram_tensor("cmask", [128, 1024], BF16, kind="ExternalInput")

    out_t = nc.dram_tensor("out", [TPC, D], F32, kind="ExternalOutput")

    agx_in = nc.dram_tensor("agx_in", [D, TPC], BF16)
    agx_out = nc.dram_tensor("agx_out", [NCO * D, TPC], BF16)
    xn_cm_d = nc.dram_tensor("xn_cm_d", [D, TPC], BF16)
    xn_tm_d = nc.dram_tensor("xn_tm_d", [TPC, D], BF16)
    q_dram = nc.dram_tensor("q_dram", [512, S], BF16)
    k_dram = nc.dram_tensor("k_dram", [512, S], BF16)
    aga_in = nc.dram_tensor("aga_in", [512, S], BF16)
    aga_out = nc.dram_tensor("aga_out", [NCO * 512, S], BF16)
    ffn_d = nc.dram_tensor("ffn_d", [TPC, D], BF16)
    dbgf = nc.dram_tensor("dbgf", [128, 8192], F32)
    _tapmap = {"agx_out": agx_out, "xn_tm_d": xn_tm_d, "q_dram": q_dram,
               "k_dram": k_dram, "aga_in": aga_in, "aga_out": aga_out,
               "ffn_d": ffn_d, "xn_cm_d": xn_cm_d, "dbgf": dbgf}
    tap_outs = []
    for tn in _TAPS:
        src = _tapmap[tn]
        tap_outs.append((nc.dram_tensor("tap_" + tn, list(src.shape), BF16,
                                        kind="ExternalOutput") if src is not dbgf else
                         nc.dram_tensor("tap_" + tn, list(src.shape), F32,
                                        kind="ExternalOutput"), src))

    SB_BASE = 16512

    def A(name, off, shape, dt):
        return nc.alloc_sbuf_tensor_at(name, shape, dt, offset=SB_BASE + off)

    xn_cm = A("xn_cm", 0, [128, G * TPC], BF16)
    v_sb = A("v_sb", 0, [128, VT * 512], BF16)
    a_sb = A("a_sb", 0, [128, G * TPC], BF16)

    wq_sb = A("wq_sb", 65536, [128, G * 512], BF16)
    wk_sb = A("wk_sb", 98304, [128, G * 512], BF16)
    wo_b0 = A("wo_b0", 65536, [128, G * 512], BF16)
    wo_b1 = A("wo_b1", 98304, [128, G * 512], BF16)
    wbr_sb = A("wbr_sb", 65536, [128, 8 * 1024], BF16)
    xnm0 = A("xnm0", 81920, [128, D], BF16)
    xnm1 = A("xnm1", 90112, [128, D], BF16)
    wall_sb = A("wall_sb", 98304, [24, D], BF16)
    xpre_f = A("xpre_f", 106496, [128, 1024], F32)
    xpre_b = A("xpre_b", 110592, [128, 1024], BF16)
    xpre_cm = A("xpre_cm", 112640, [128, 1024], BF16)
    dso = 114688
    logits = A("logits", dso, [128, 8 * 24], F32); dso += 768
    zv = A("zv", dso, [128, 128], F32); dso += 512
    s1b = A("s1b", dso, [128, 128], F32); dso += 512
    s2b = A("s2b", dso, [128, 128], F32); dso += 512
    e1b = A("e1b", dso, [128, 128], F32); dso += 512
    sub_ = A("sub_", dso, [128, 32], F32); dso += 128
    svb_ = A("svb_", dso, [128, 32], F32); dso += 128
    ulb = A("ulb", dso, [128, 32], F32); dso += 128
    vlb = A("vlb", dso, [128, 32], F32); dso += 128
    hres = A("hres", dso, [128, 128], F32); dso += 512
    i16b = A("i16b", dso, [128, 128], F32); dso += 512
    hpre = A("hpre", dso, [128, 32], F32); dso += 128
    hpost = A("hpost", dso, [128, 32], F32); dso += 128
    hwork = A("hwork", dso, [128, 32], F32); dso += 128
    rsum8 = A("rsum8", dso, [128, 8], F32); dso += 32
    rcp8 = A("rcp8", dso, [128, 8], F32); dso += 32
    wallT = A("wallT", dso, [128, G * 24], BF16); dso += 1536
    basesb = A("basesb", dso, [128, 24], F32); dso += 96
    gb = A("gb", dso, [128, 2], F32); dso += 32
    assert dso <= 131072, dso

    cos_sb = A("cos_sb", 131072, [128, S], BF16)
    sin_sb = A("sin_sb", 147456, [128, S], BF16)
    sv0 = A("sv0", 131072, [128, G * 128], BF16)
    sv1 = A("sv1", 139264, [128, G * 128], BF16)
    kh0 = A("kh0", 131072, [128, S], BF16)
    kh1 = A("kh1", 147456, [128, S], BF16)
    xs0 = A("xs0", 131072, [128, 512], F32)
    xs1 = A("xs1", 133120, [128, 512], F32)
    fs0 = A("fs0", 135168, [128, 512], BF16)
    fs1 = A("fs1", 136192, [128, 512], BF16)
    os0 = A("os0", 137216, [128, 512], F32)
    os1 = A("os1", 139264, [128, 512], F32)
    sv3 = A("sv3", 81920, [128, G * 128], BF16)
    p2 = A("p2", 192512, [128, 512], BF16)

    xb0 = A("xb0", 163840, [128, 2048], F32)
    xb1 = A("xb1", 172032, [128, 2048], F32)
    xnt0 = A("xnt0", 180224, [128, D], BF16)
    xnt1 = A("xnt1", 188416, [128, D], BF16)
    mv0 = A("mv0", 163840, [128, 16 * 512], BF16)
    mv1 = A("mv1", 180224, [128, 16 * 512], BF16)
    wv_sb = A("wv_sb", 163840, [128, G * 512], BF16)
    co = 163840
    p0 = A("p0", co, [128, 512], BF16); co += 1024
    p1 = A("p1", co, [128, 512], BF16); co += 1024
    qt0 = A("qt0", co, [128, 512], BF16); co += 1024
    qt1 = A("qt1", co, [128, 512], BF16); co += 1024
    at0 = A("at0", co, [128, 512], BF16); co += 1024
    at1 = A("at1", co, [128, 512], BF16); co += 1024
    rec0 = A("rec0", co, [128, 512], F32); co += 2048
    rec1 = A("rec1", co, [128, 512], F32); co += 2048
    cmask_sb = A("cmask_sb", co, [128, 1024], BF16); co += 2048
    sv2 = A("sv2", co, [128, G * 128], BF16); co += 8192
    ffnt = A("ffnt", co, [128, D], BF16); co += 8192
    assert co <= 196608, co

    so = 196608
    ones_sb = A("ones_sb", so, [128, 128], BF16); so += 256
    eye_sb = A("eye_sb", so, [128, 128], BF16); so += 256
    consts = A("consts", so, [128, 8], F32); so += 32
    stats = A("stats", so, [128, 8], F32); so += 32
    onesf = A("onesf", so, [1, 128], F32); so += 512
    invf = A("invf", so, [1, 128], F32); so += 512
    iotc = A("iotc", so, [1, 512], F32); so += 2048
    frq0 = A("frq0", so, [128, 512], F32); so += 2048
    frq1 = A("frq1", so, [128, 512], F32); so += 2048
    kint = A("kint", so, [128, 512], I32); so += 2048
    prt = A("prt", so, [128, 512], F32); so += 2048
    rt1 = A("rt1", so, [128, 512], F32); so += 2048
    qkb0 = A("qkb0", so, [128, 512], BF16); so += 1024
    qkb1 = A("qkb1", so, [128, 512], BF16); so += 1024
    gatesb = A("gatesb", so, [1, 4], F32); so += 32
    gcat = A("gcat", so, [1, 2], F32); so += 32
    basesr = A("basesr", so, [1, 24], F32); so += 96
    sxn = A("sxn", so, [128, 2], F32); so += 32
    rmsr8 = A("rmsr8", so, [128, 8], F32); so += 32
    assert so <= 212800, so

    PB = [es.enter_context(nc.psum_tensor("pb%d" % i, [128, 512], F32)) for i in range(8)]
    ptr_a = PB[1][:, :].bitcast(BF16)[:, 0:128]
    ptr_b = PB[2][:, :].bitcast(BF16)[:, 0:128]

    sL = Ctr(es, nc, "sL")
    sG = Ctr(es, nc, "sG")
    sCC = Ctr(es, nc, "sCC")
    sPE = Ctr(es, nc, "sPE")
    sACT = Ctr(es, nc, "sACT")
    sDVE = Ctr(es, nc, "sDVE")
    sPL = Ctr(es, nc, "sPL")
    sL2 = Ctr(es, nc, "sL2")
    sG2 = Ctr(es, nc, "sG2")

    def dma(queue, ctr, outf, inf, waits=()):
        waits = list(waits)
        if _SERIAL or cur_stage[0] == 'dlog' or (_SERIAL_STAGE is not None and cur_stage[0] == _SERIAL_STAGE):
            mm = _SERIAL_MASK
            if 'P' in mm: waits.append((sPE.sem, sPE.n))
            if 'A' in mm: waits.append((sACT.sem, sACT.n))
            if 'D' in mm: waits.append((sDVE.sem, sDVE.n))
        ctr.n += 16
        def f(e, outf=outf, inf=inf, waits=tuple(waits), sem=ctr.sem):
            for (s, v) in waits:
                if v > 0:
                    e.wait_ge(s, v)
            e.dma_start(out=outf(), in_=inf()).then_inc(sem, 16)
        P.on(queue, f)
        return ctr.n

    cur_stage = [None]

    def step(eng, fn, waits=(), inc=None):
        waits = list(waits)
        if _SERIAL or cur_stage[0] == 'dlog' or (_SERIAL_STAGE is not None and cur_stage[0] == _SERIAL_STAGE):
            mm = _SERIAL_MASK
            if 'P' in mm: waits.append((sPE.sem, sPE.n))
            if 'A' in mm: waits.append((sACT.sem, sACT.n))
            if 'D' in mm: waits.append((sDVE.sem, sDVE.n))
            if 'L' in mm: waits.append((sL.sem, sL.n))
            if 'G' in mm: waits.append((sG.sem, sG.n))
        if inc is not None:
            inc.n += 1
        def f(e, fn=fn, waits=tuple(waits), inc=inc):
            for (s, v) in waits:
                if v > 0:
                    e.wait_ge(s, v)
            ins = fn(e)
            if inc is not None:
                ins.then_inc(inc.sem, 1)
        P.on(eng, f)
        return inc.n if inc is not None else 0

    W = lambda ctr, v: (ctr.sem, v)

    dma("sync", sL, lambda: eye_sb[:, :], lambda: eye_in.ap())
    L_const = dma("sync", sL, lambda: ones_sb[:, :], lambda: ones_in.ap())
    dma("sync", sL2, lambda: basesr[:, :], lambda: bases_in.ap())
    dma("sync", sL2, lambda: gatesb[0:1, 0:1], lambda: gate_in.ap())
    L2_const = dma("sync", sL2, lambda: invf[:, :], lambda: iota_d_in.ap())

    def memsets(e):
        e.memset(consts[:, 0:1], EXP_BIAS)
        e.memset(consts[:, 1:2], SCALE)
        e.memset(consts[:, 2:3], LN_EPS)
        e.memset(consts[:, 3:4], -0.5)
        e.memset(consts[:, 4:5], RMS_EPS)
        e.memset(consts[:, 5:6], -1.0)
        return e.memset(onesf[:, :], 1.0)
    P_const = step("gpsimd", memsets, inc=sPL)

    A_invf = step("scalar",
                  lambda e: e.activation(invf[:, :], invf[:, :], AF.Exp,
                                         scale=-float(np.log(10000.0) / 64.0)),
                  waits=[W(sL2, L2_const), W(sPL, P_const)], inc=sACT)

    G_wq = dma("gpsimd", sG,
               lambda: wq_sb[:, :].rearrange("p (g n) -> p g n", g=G),
               lambda: wq_in.ap().rearrange("(g p) n -> p g n", p=128))
    G_wqk = dma("gpsimd", sG,
                lambda: wk_sb[:, :].rearrange("p (g n) -> p g n", g=G),
                lambda: wk_in.ap().rearrange("(g p) n -> p g n", p=128))

    ln_xn_done = [0] * NTT
    rmsr_done = [0] * NTT
    ln_ev_done = [0] * NTT
    ln_st_done = [0] * NTT
    bank_ev = {}
    x_ld = {}

    for tt in range(NTT):
        for half in range(2):
            xb = xb0 if half == 0 else xb1
            w = [W(sDVE, ln_xn_done[tt - 1])] if tt > 0 else []
            x_ld[(tt, half)] = dma(
                "sync", sL,
                lambda xb=xb: xb[:, :],
                lambda tt=tt, half=half: x_in.ap()[tt * 128:(tt + 1) * 128,
                                                  half * 2048:(half + 1) * 2048],
                waits=w)

        xnt = xnt0 if tt % 2 == 0 else xnt1
        def stats_f(e, tt=tt, xnt=xnt):
            for half, xb in ((0, xb0), (1, xb1)):
                e.scalar_tensor_tensor(xnt[:, half * 2048:(half + 1) * 2048],
                                       xb[:, :], 1.0, xb[:, :], ALU.bypass, ALU.mult,
                                       accum_out=stats[:, half:half + 1])
                e.tensor_reduce(stats[:, 2 + half:3 + half], xb[:, :], AX.X, ALU.add)
            e.drain()
            e.tensor_tensor(stats[:, 2:3], stats[:, 2:3], stats[:, 3:4], ALU.add)
            e.tensor_tensor(stats[:, 0:1], stats[:, 0:1], stats[:, 1:2], ALU.add)
            e.drain()
            e.tensor_scalar(stats[:, 2:3], stats[:, 2:3], 1.0 / D, None, ALU.mult)
            e.drain()
            e.tensor_tensor(stats[:, 3:4], stats[:, 2:3], stats[:, 2:3], ALU.mult)
            e.drain()
            return e.tensor_scalar(stats[:, 1:2], stats[:, 0:1], 1.0 / D,
                                   stats[:, 3:4], ALU.mult, ALU.subtract)
        w_st = [W(sL, x_ld[(tt, 0)]), W(sL, x_ld[(tt, 1)]), W(sPL, P_const)]
        if tt > 0:
            w_st.append(W(sACT, rmsr_done[tt - 1]))
        if tt > 1:
            w_st += [W(sDVE, ln_ev_done[tt - 2]), W(sG, ln_st_done[tt - 2])]
        var_done = step("vector", stats_f, waits=w_st, inc=sDVE)

        def rstd_f(e):
            e.activation(stats[:, 5:6], stats[:, 1:2], AF.Ln, bias=consts[:, 2:3])
            e.drain()
            return e.activation(stats[:, 4:5], stats[:, 5:6], AF.Exp, scale=consts[:, 3:4])
        rstd_done = step("scalar", rstd_f, waits=[W(sDVE, var_done)], inc=sACT)

        def xn_f(e, xnt=xnt):
            e.tensor_scalar(xnt[:, 0:2048], xb0[:, :], stats[:, 2:3], stats[:, 4:5],
                            ALU.subtract, ALU.mult)
            return e.tensor_scalar(xnt[:, 2048:4096], xb1[:, :], stats[:, 2:3],
                                   stats[:, 4:5], ALU.subtract, ALU.mult)
        ln_xn_done[tt] = step("vector", xn_f, waits=[W(sACT, rstd_done)], inc=sDVE)

        def msq_f(e, tt=tt):
            e.tensor_scalar(stats[:, 0:1], stats[:, 1:2], LN_EPS, None, ALU.add)
            e.drain()
            e.reciprocal(stats[:, 3:4], stats[:, 0:1])
            e.drain()
            return e.tensor_tensor(stats[:, 0:1], stats[:, 1:2], stats[:, 3:4], ALU.mult)
        msq_done = step("vector", msq_f, waits=[W(sACT, rstd_done)], inc=sDVE)

        def rmsr_f(e, tt=tt):
            e.activation(stats[:, 3:4], stats[:, 0:1], AF.Ln, bias=consts[:, 4:5])
            e.drain()
            return e.activation(rmsr8[:, tt:tt + 1], stats[:, 3:4], AF.Exp,
                                scale=consts[:, 3:4])
        rmsr_done[tt] = step("scalar", rmsr_f, waits=[W(sDVE, msq_done)], inc=sACT)

        ln_st_done[tt] = dma("gpsimd", sG,
                             lambda tt=tt: xn_tm_d.ap()[tt * 128:(tt + 1) * 128, :],
                             lambda xnt=xnt: xnt[:, :],
                             waits=[W(sDVE, ln_xn_done[tt])])

        for g_ in range(G):
            bank = g_ % 2
            ptr = ptr_a if bank == 0 else ptr_b
            w_tr = [W(sDVE, ln_xn_done[tt])]
            if bank_ev.get(bank, 0) > 0:
                w_tr.append(W(sDVE, bank_ev[bank]))
            step("tensor",
                 lambda e, xnt=xnt, g_=g_, ptr=ptr: e.transpose(
                     ptr, xnt[:, g_ * 128:(g_ + 1) * 128], eye_sb[:, :]),
                 waits=w_tr, inc=sPE)
            tr_cnt = sPE.n
            bank_ev[bank] = step(
                "vector",
                lambda e, tt=tt, g_=g_, ptr=ptr: e.tensor_copy(
                    xn_cm[:, g_ * TPC + tt * 128: g_ * TPC + (tt + 1) * 128], ptr),
                waits=[W(sPE, tr_cnt)], inc=sDVE)
        ln_ev_done[tt] = sDVE.n

    ln_all_ev = sDVE.n
    G_agx = dma("gpsimd", sG,
                lambda: agx_in.ap().rearrange("(g p) t -> p g t", p=128),
                lambda: xn_cm[:, :].rearrange("p (g t) -> p g t", g=G),
                waits=[W(sDVE, ln_all_ev)])
    G_xcm = dma("gpsimd", sG,
                lambda: xn_cm_d.ap().rearrange("(g p) t -> p g t", p=128),
                lambda: xn_cm[:, :].rearrange("p (g t) -> p g t", g=G))

    def agx_f(e):
        e.wait_ge(sG.sem, G_xcm)
        return e.collective_compute("AllGather", ALU.bypass,
                                    replica_groups=[list(range(NCO))],
                                    ins=[agx_in.ap()], outs=[agx_out.ap()])
    sCC.n += 1
    P.on("gpsimd", lambda e: agx_f(e).then_inc(sCC.sem, 1))
    CC_X = sCC.n

    tab_dve = []
    for tcx in range(TC):
        w = [W(sDVE, tab_dve[tcx - 1][1])] if tcx else []
        iot_ld = dma("sync", sL2, lambda: iotc[:, :],
                     lambda tcx=tcx: iota_t_in.ap()[:, tcx * 512:(tcx + 1) * 512],
                     waits=w)
        w_mm = [W(sL2, iot_ld), W(sACT, A_invf)]
        if tcx:
            w_mm.append(W(sDVE, tab_dve[tcx - 1][1]))
        frq_pe = step("tensor",
                      lambda e: e.matmul(PB[0][:, :], invf[:, :], iotc[:, :],
                                         start=True, stop=True),
                      waits=w_mm, inc=sPE)

        def redu_f(e):
            e.tensor_scalar(kint[:, :], PB[0][:, :], float(1.0 / (2 * np.pi)), None, ALU.mult)
            e.drain()
            e.tensor_copy(frq0[:, :], kint[:, :])
            e.drain()
            e.scalar_tensor_tensor(frq0[:, :], frq0[:, :], -float(2 * np.pi),
                                   PB[0][:, :], ALU.mult, ALU.add)
            e.tensor_scalar(frq1[:, :], PB[0][:, :], float(np.pi / 2), None, ALU.add)
            e.drain()
            e.tensor_scalar(kint[:, :], frq1[:, :], float(1.0 / (2 * np.pi)), None, ALU.mult)
            e.drain()
            e.tensor_copy(rt1[:, :], kint[:, :])
            e.drain()
            return e.scalar_tensor_tensor(frq1[:, :], rt1[:, :], -float(2 * np.pi),
                                          frq1[:, :], ALU.mult, ALU.add)
        rr = step("vector", redu_f, waits=[W(sPE, frq_pe)], inc=sDVE)
        tab_dve.append((rr, rr))

        def sin_f(e, tcx=tcx):
            e.activation(sin_sb[:, tcx * 512:(tcx + 1) * 512], frq0[:, :], AF.Sin)
            return e.activation(cos_sb[:, tcx * 512:(tcx + 1) * 512], frq1[:, :], AF.Sin)
        sa = step("scalar", sin_f, waits=[W(sDVE, rr)], inc=sACT)
        tab_dve[tcx] = (rr, rr, sa)
    tab_act = sACT.n
    tab_done = step("vector",
                    lambda e: e.tensor_scalar(sin_sb[0:64, :], sin_sb[0:64, :],
                                              -1.0, None, ALU.mult),
                    waits=[W(sACT, tab_act)], inc=sDVE)

    WQK = (wq_sb, wk_sb)
    mov_ld = {}
    mov_free = {}
    rope_m1 = {}
    rope_out = {}
    qk_stores = {}
    for tcx in range(TC):
        for half in range(2):
            idx = tcx * 2 + half
            mv = mv0 if half == 0 else mv1
            w = [W(sCC, CC_X)]
            if idx >= 2:
                w.append(W(sPE, mov_free[idx - 2]))
            r, sh = tcx // 2, (tcx % 2) * 512
            mov_ld[idx] = dma(
                "sync", sL,
                lambda mv=mv: mv[:, :].rearrange("p (g t) -> p g t", g=16),
                lambda r=r, half=half, sh=sh: agx_out.ap()
                    .rearrange("(r g p) t -> r p g t", g=G, p=128)[r]
                    [:, half * 16:(half + 1) * 16, sh:sh + 512],
                waits=w)

        for half in range(2):
            idx = tcx * 2 + half
            mv = mv0 if half == 0 else mv1
            for gg in range(16):
                g_ = half * 16 + gg
                for j in range(8):
                    w_mm = []
                    if g_ == 0 and j == 0:
                        w_mm = [W(sL, mov_ld[idx]), W(sG, G_wqk)]
                        if tcx == 0:
                            w_mm += [W(sDVE, tab_done), W(sDVE, ln_all_ev)]
                    elif gg == 0 and j == 0:
                        w_mm = [W(sL, mov_ld[idx])]
                    if g_ == 0 and tcx > 0:
                        w_mm.append(W(sDVE, rope_m1[(tcx - 1, j)]))
                    wsb = WQK[j // 4]
                    jj = j % 4
                    last = (g_ == G - 1)
                    step("tensor",
                         lambda e, wsb=wsb, g_=g_, jj=jj, j=j, mv=mv, gg=gg, first=(g_ == 0), last=last:
                         e.matmul(PB[j][:, :],
                                  wsb[:, g_ * 512 + jj * 128: g_ * 512 + (jj + 1) * 128],
                                  mv[:, gg * 512:(gg + 1) * 512],
                                  start=first, stop=last),
                         waits=w_mm, inc=(sPE if (last or gg == 15) else None))
            mov_free[idx] = sPE.n
        qk_pe_last = sPE.n

        for j in range(8):
            qkb = qkb0 if j % 2 == 0 else qkb1
            w_r = [W(sPE, qk_pe_last - 8 + j + 1)]
            prev = qk_stores.get((tcx, j - 2)) or (qk_stores.get((tcx - 1, j + 6)) if tcx else None)
            if prev:
                w_r.append(W(sG, prev))

            def rope_f1(e, j=j, tcx=tcx):
                e.tensor_copy(prt[0:64, :], PB[j][64:128, :])
                e.tensor_copy(prt[64:128, :], PB[j][0:64, :])
                e.drain()
                return e.tensor_tensor(rt1[:, :], PB[j][:, :],
                                       cos_sb[:, tcx * 512:(tcx + 1) * 512], ALU.mult)
            rope_m1[(tcx, j)] = step("vector", rope_f1, waits=w_r, inc=sDVE)

            def rope_f2(e, j=j, tcx=tcx, qkb=qkb):
                e.tensor_tensor(frq0[:, :], prt[:, :],
                                sin_sb[:, tcx * 512:(tcx + 1) * 512], ALU.mult)
                e.drain()
                return e.tensor_tensor(qkb[:, :], rt1[:, :], frq0[:, :], ALU.add)
            rope_out[(tcx, j)] = step("vector", rope_f2, inc=sDVE)

            dest = q_dram if j < 4 else k_dram
            qk_stores[(tcx, j)] = dma(
                "gpsimd", sG,
                lambda dest=dest, j=j, tcx=tcx: dest.ap()[(j % 4) * 128:(j % 4 + 1) * 128,
                                                          tcx * 512:(tcx + 1) * 512],
                lambda qkb=qkb: qkb[:, :],
                waits=[W(sDVE, rope_out[(tcx, j)])])
    qk_all_pe = sPE.n
    rope_all = sDVE.n
    qk_all_st = sG.n

    G_wv = dma("gpsimd", sG,
               lambda: wv_sb[:, :].rearrange("p (g n) -> p g n", g=G),
               lambda: wv_in.ap().rearrange("(g p) n -> p g n", p=128),
               waits=[W(sPE, qk_all_pe)])

    sv_ld = {}
    v_pe = {}
    v_ev = {}
    for vt in range(VT):
        sv = sv0 if vt % 2 == 0 else sv1
        w = []
        if vt == 0:
            w = [W(sPE, qk_all_pe), W(sDVE, rope_all)]
        if vt >= 2:
            w.append(W(sPE, v_pe[vt - 2]))
        r, tt_ = vt // 8, vt % 8
        sv_ld[vt] = dma(
            "sync", sL,
            lambda sv=sv: sv[:, :].rearrange("p (g t) -> p g t", g=G),
            lambda r=r, tt_=tt_: agx_out.ap()
                .rearrange("(r g p) t -> r p g t", g=G, p=128)[r]
                [:, :, tt_ * 128:(tt_ + 1) * 128],
            waits=w)

        pb = PB[vt % 2]
        w_mm = [W(sL, sv_ld[vt]), W(sG, G_wv)]
        if vt == 0:
            w_mm += [W(sG, G_xcm), W(sDVE, rope_all)]
        if vt >= 2:
            w_mm.append(W(sDVE, v_ev[vt - 2]))

        def vmm_f(e, sv=sv, pb=pb):
            ins = None
            for g_ in range(G):
                ins = e.matmul(pb[:, :], sv[:, g_ * 128:(g_ + 1) * 128],
                               wv_sb[:, g_ * 512:(g_ + 1) * 512],
                               start=(g_ == 0), stop=(g_ == G - 1))
            return ins
        v_pe[vt] = step("tensor", vmm_f, waits=w_mm, inc=sPE)
        v_ev[vt] = step("vector",
                        lambda e, vt=vt, pb=pb: e.tensor_copy(
                            v_sb[:, vt * 512:(vt + 1) * 512], pb[:, :]),
                        waits=[W(sPE, v_pe[vt])], inc=sDVE)
    v_all_pe = sPE.n
    v_all = sDVE.n

    G_wbr = dma("gpsimd", sG,
                lambda: wbr_sb[:, :].rearrange("p (g n) -> p g n", g=8),
                lambda: wbr_in.ap().rearrange("(g p) n -> p g n", p=128),
                waits=[W(sPE, qk_all_pe)])
    G_wall = dma("gpsimd", sG2, lambda: wall_sb[:, :], lambda: wall_in.ap())
    G_ffnw = sG.n

    kh_ld = {}
    L_cmask = dma("sync", sL, lambda: cmask_sb[:, :], lambda: cmask_in.ap(),
                  waits=[W(sPE, v_all_pe), W(sDVE, rope_all)])
    kh_ld[0] = dma("sync", sL, lambda: kh0[:, :], lambda: k_dram.ap()[0:128, :],
                   waits=[W(sG, qk_all_st)])

    exp_on_bank = {}
    pv_on_p = {}
    qt_ld = {}
    qt_free = {}
    kh_free = {}
    at_store = {}
    at_cons = 0
    po_free = 0
    rec_free = 0
    c_entry = 0
    it = 0
    for h in range(NH):
        kh = kh0 if h % 2 == 0 else kh1
        if h + 1 < NH:
            khn = kh1 if (h + 1) % 2 == 1 else kh0
            w = [W(sPE, kh_free[h - 1])] if h >= 1 else [W(sPE, v_all_pe), W(sDVE, rope_all)]
            kh_ld[h + 1] = dma("sync", sL, lambda khn=khn: khn[:, :],
                               lambda h=h: k_dram.ap()[(h + 1) * 128:(h + 2) * 128, :],
                               waits=w)
        for qb in range(QB):
            qt = qt0 if it % 2 == 0 else qt1
            at = at0 if it % 2 == 0 else at1
            nkc = 4 * (qb + 1)
            w = [W(sG, qk_all_st)] if it == 0 else []
            if it >= 2:
                w.append(W(sPE, qt_free[it - 2]))
            qt_ld[it] = dma("sync", sL, lambda qt=qt: qt[:, :],
                            lambda h=h, qb=qb: q_dram.ap()[h * 128:(h + 1) * 128,
                                                           qb * 512:(qb + 1) * 512],
                            waits=w)
            mm1_last = 0
            SBANKS = (PB[0], PB[1], PB[4])
            PBUFS = (p0, p1, p2)
            exp_cnt = {}
            p_rdy = {}

            def emit_pv(kc):
                pbuf = PBUFS[kc % 3]
                eng, val = p_rdy[kc]
                w2 = [(eng, val)]
                if kc == 0 and po_free > 0:
                    w2.append(W(sDVE, po_free))
                step("tensor",
                     lambda e, kc=kc, h=h, pbuf=pbuf, first=(kc == 0), last=(kc == nkc - 1):
                     e.matmul(PB[2][:, :],
                              v_sb[:, kc * 512 + h * 128: kc * 512 + (h + 1) * 128],
                              pbuf[:, :], start=first, stop=last),
                     waits=w2)
                w3 = []
                if kc == 0 and rec_free > 0:
                    w3 = [W(sACT, rec_free)]
                pv_on_p[kc % 3] = step(
                    "tensor",
                    lambda e, pbuf=pbuf, first=(kc == 0), last=(kc == nkc - 1):
                    e.matmul(PB[3][:, :], ones_sb[:, :], pbuf[:, :],
                             start=first, stop=last),
                    waits=w3, inc=sPE)

            for kc in range(nkc):
                rb = kc % 3
                pb_s = SBANKS[rb]
                pbuf = PBUFS[rb]
                diag = kc >= nkc - 4
                w_mm1 = []
                if kc == 0:
                    w_mm1 = [W(sL, qt_ld[it]), W(sL, kh_ld[h])]
                    if it == 0:
                        w_mm1 += [W(sDVE, v_all), W(sDVE, rope_all)]
                if exp_on_bank.get(rb, 0) > 0:
                    w_mm1.append(W(sACT, exp_on_bank[rb]))
                mm1 = step("tensor",
                           lambda e, pb_s=pb_s, kh=kh, kc=kc, qt=qt:
                           e.matmul(pb_s[:, :], kh[:, kc * 128:(kc + 1) * 128],
                                    qt[:, :], start=True, stop=True),
                           waits=w_mm1, inc=sPE)
                mm1_last = mm1
                if kc >= 2:
                    emit_pv(kc - 2)
                w_exp = [W(sPE, mm1)]
                if pv_on_p.get(rb, 0) > 0:
                    w_exp.append(W(sPE, pv_on_p[rb]))
                exp_on_bank[rb] = step(
                    "scalar",
                    lambda e, pbuf=pbuf, pb_s=pb_s: e.activation(
                        pbuf[:, :], pb_s[:, :], AF.Exp,
                        bias=consts[:, 0:1], scale=consts[:, 1:2]),
                    waits=w_exp, inc=sACT)
                if diag:
                    i_ = kc - (nkc - 4)
                    moff = 384 - 128 * i_
                    p_rdy[kc] = (sDVE.sem, step(
                        "vector",
                        lambda e, pbuf=pbuf, moff=moff: e.tensor_tensor(
                            pbuf[:, :], pbuf[:, :], cmask_sb[:, moff:moff + 512], ALU.mult),
                        waits=[W(sACT, exp_on_bank[rb]), W(sL, L_cmask)], inc=sDVE))
                else:
                    p_rdy[kc] = (sACT.sem, exp_on_bank[rb])
            emit_pv(nkc - 2)
            emit_pv(nkc - 1)
            qt_free[it] = mm1_last
            pv_last = sPE.n

            def recf(e):
                e.activation(rec0[:, :], PB[3][:, :], AF.Ln)
                e.drain()
                return e.activation(rec1[:, :], rec0[:, :], AF.Exp, scale=consts[:, 5:6])
            w_rec = [W(sPE, pv_last)]
            if at_cons > 0:
                w_rec.append(W(sDVE, at_cons))
            rec_free = step("scalar", recf, waits=w_rec, inc=sACT)

            w_at = [W(sACT, rec_free), W(sPE, pv_last)]
            if at_store.get(it - 2):
                w_at.append(W(sG, at_store[it - 2]))
            at_cons = step("vector",
                           lambda e, at=at: e.tensor_tensor(at[:, :], PB[2][:, :],
                                                            rec1[:, :], ALU.mult),
                           waits=w_at, inc=sDVE)
            po_free = at_cons
            at_store[it] = dma("gpsimd", sG,
                               lambda h=h, qb=qb: aga_in.ap()[h * 128:(h + 1) * 128,
                                                              qb * 512:(qb + 1) * 512],
                               lambda at=at: at[:, :],
                               waits=[W(sDVE, at_cons)])
            it += 1
        kh_free[h] = qt_free[it - 1]
    c_pe_last = sPE.n
    c_dve_last = sDVE.n
    c_act_last = sACT.n
    at_all = sG.n

    def aga_f(e):
        e.wait_ge(sG.sem, at_all)
        return e.collective_compute("AllGather", ALU.bypass,
                                    replica_groups=[list(range(NCO))],
                                    ins=[aga_in.ap()], outs=[aga_out.ap()])
    sCC.n += 1
    P.on("gpsimd", lambda e: aga_f(e).then_inc(sCC.sem, 1))
    CC_A = sCC.n

    d_entry_w = [W(sPE, c_pe_last), W(sACT, c_act_last), W(sDVE, c_dve_last),
                 W(sG, G_ffnw), W(sG2, G_wall)]
    wallT_ev = 0
    for g_ in range(G):
        ptr = ptr_a if g_ % 2 == 0 else ptr_b
        w_tr = list(d_entry_w) if g_ == 0 else []
        if g_ >= 2:
            w_tr.append(W(sDVE, wallT_ev - 1))
        step("tensor",
             lambda e, g_=g_, ptr=ptr: e.transpose(
                 ptr[:, 0:24], wall_sb[0:24, g_ * 128:(g_ + 1) * 128], eye_sb[0:24, 0:24]),
             waits=w_tr, inc=sPE)
        trc = sPE.n
        w_ev = [W(sPE, trc)] + ([W(sDVE, c_dve_last)] if g_ == 0 else [])
        wallT_ev = step("vector",
                        lambda e, g_=g_, ptr=ptr: e.tensor_copy(
                            wallT[:, g_ * 24:(g_ + 1) * 24], ptr[:, 0:24]),
                        waits=w_ev, inc=sDVE)

    bb_pe = step("tensor",
                 lambda e: e.matmul(PB[3][:, 0:24], onesf[:, :], basesr[:, :],
                                    start=True, stop=True),
                 waits=[W(sACT, c_act_last), W(sL2, L2_const)], inc=sPE)
    bb_ev = step("vector", lambda e: e.tensor_copy(basesb[:, :], PB[3][:, 0:24]),
                 waits=[W(sPE, bb_pe)], inc=sDVE)

    gact = step("scalar",
                lambda e: e.activation(gatesb[0:1, 1:2], gatesb[0:1, 0:1], AF.Exp,
                                       scale=consts[0:1, 5:6]),
                waits=[W(sL2, L2_const), W(sPL, P_const)], inc=sACT)

    def gate_f(e):
        e.tensor_scalar(gatesb[0:1, 2:3], gatesb[0:1, 1:2], 1.0, None, ALU.add)
        e.drain()
        e.reciprocal(gatesb[0:1, 3:4], gatesb[0:1, 2:3])
        e.drain()
        e.tensor_copy(gcat[0:1, 0:1], gatesb[0:1, 3:4])
        return e.tensor_tensor(gcat[0:1, 1:2], gatesb[0:1, 1:2], gatesb[0:1, 3:4],
                               ALU.mult)
    g_dve = step("vector", gate_f, waits=[W(sACT, gact)], inc=sDVE)
    gb_pe = step("tensor",
                 lambda e: e.matmul(PB[3][:, 32:34], onesf[:, :], gcat[:, :],
                                    start=True, stop=True),
                 waits=[W(sDVE, g_dve), W(sDVE, bb_ev)], inc=sPE)
    gb_ev = step("vector", lambda e: e.tensor_copy(gb[:, :], PB[3][:, 32:34]),
                 waits=[W(sPE, gb_pe)], inc=sDVE)
    def i16_f(e):
        e.memset(i16b[:, :], 0.0)
        e.drain()
        ins = None
        for pos in (0, 5, 10, 15):
            ins = e.tensor_copy(
                i16b[:, :].rearrange("p (t j) -> p t j", j=16)[:, :, pos],
                gb[:, 1:2].broadcast_to((128, 8)))
        return ins
    i16_done = step("vector", i16_f, waits=[W(sDVE, gb_ev)], inc=sDVE)

    cur_stage[0] = 'dlog'
    sv2_free = {}
    lg_ev = 0
    for tt in range(NTT):
        svb = sv2 if tt % 2 == 0 else sv3
        w = [W(sG, G_xcm)]
        if tt < 2:
            w.append(W(sPE, qk_all_pe))
        if sv2_free.get(tt - 2):
            w.append(W(sPE, sv2_free[tt - 2]))
        ld_sv = dma("sync", sL,
                    lambda svb=svb: svb[:, :].rearrange("p (g t) -> p g t", g=G),
                    lambda tt=tt: xn_cm_d.ap()
                        .rearrange("(g p) t -> p g t", p=128)[:, :, tt * 128:(tt + 1) * 128],
                    waits=w)

        def lg_mm(e, svb=svb):
            ins = None
            for g_ in range(G):
                ins = e.matmul(PB[0][:, 0:24], svb[:, g_ * 128:(g_ + 1) * 128],
                               wallT[:, g_ * 24:(g_ + 1) * 24],
                               start=(g_ == 0), stop=(g_ == G - 1))
            return ins
        w_lg = [W(sL, ld_sv), W(sDVE, wallT_ev)]
        if tt == 0:
            w_lg += [W(sACT, c_act_last)]
        if lg_ev > 0:
            w_lg.append(W(sDVE, lg_ev))
        sv2_free[tt] = step("tensor", lg_mm, waits=w_lg, inc=sPE)
        if "dbgf" in _TAPS and tt == 0:
            dma("gpsimd", sG, lambda svb=svb: dbgf.ap()[:, 4096:8192],
                lambda svb2=svb: svb2[:, :], waits=list(w_lg))

        def lg_ev_f(e, tt=tt):
            e.tensor_scalar(logits[:, tt * 24:(tt + 1) * 24], PB[0][:, 0:24],
                            rmsr8[:, tt:tt + 1], None, ALU.mult)
            e.drain()
            return e.tensor_tensor(logits[:, tt * 24:(tt + 1) * 24],
                                   logits[:, tt * 24:(tt + 1) * 24], basesb[:, :], ALU.add)
        lg_ev = step("vector", lg_ev_f,
                     waits=[W(sPE, sv2_free[tt]), W(sACT, rmsr_done[NTT - 1]),
                            W(sDVE, bb_ev)], inc=sDVE)
    cur_stage[0] = 'dsink'
    lg_all = sDVE.n

    lg4 = logits[:, :].rearrange("p (t n) -> p t n", n=24)
    zv4 = zv[:, :].rearrange("p (t i j) -> p t i j", i=4, j=4)
    s14 = s1b[:, :].rearrange("p (t i j) -> p t i j", i=4, j=4)
    s24 = s2b[:, :].rearrange("p (t i j) -> p t i j", i=4, j=4)
    e14 = e1b[:, :].rearrange("p (t i j) -> p t i j", i=4, j=4)
    su3 = sub_[:, :].rearrange("p (t i) -> p t i", i=4)
    sv3 = svb_[:, :].rearrange("p (t j) -> p t j", j=4)
    ul3 = ulb[:, :].rearrange("p (t i) -> p t i", i=4)
    vl3 = vlb[:, :].rearrange("p (t j) -> p t j", j=4)

    def sink_init(e):
        e.tensor_scalar(zv[:, :].rearrange("p (t i j) -> p t i j", i=4, j=4),
                        lg4[:, :, 0:16].rearrange("p t (i j) -> p t i j", j=4),
                        10.0, None, ALU.mult)
        return e.memset(vlb[:, :], 0.0)
    sk = step("vector", sink_init, waits=[W(sDVE, lg_all)], inc=sDVE)

    for itr in range(SINK_ITERS):
        sk = step("vector",
                  lambda e: e.tensor_tensor(s14, zv4, vl3.unsqueeze(2).broadcast_to(
                      (128, 8, 4, 4)), ALU.subtract),
                  waits=[W(sDVE, sk)] if itr == 0 else [W(sACT, sk)], inc=sDVE)
        sk = step("scalar", lambda e: e.activation(e1b[:, :], s1b[:, :], AF.Exp),
                  waits=[W(sDVE, sk)], inc=sACT)
        sk = step("vector", lambda e: e.tensor_reduce(su3, e14, AX.X, ALU.add),
                  waits=[W(sACT, sk)], inc=sDVE)
        sk = step("scalar", lambda e: e.activation(ulb[:, :], sub_[:, :], AF.Ln),
                  waits=[W(sDVE, sk)], inc=sACT)
        sk = step("vector",
                  lambda e: e.tensor_tensor(s24, zv4, ul3.unsqueeze(3).broadcast_to(
                      (128, 8, 4, 4)), ALU.subtract),
                  waits=[W(sACT, sk)], inc=sDVE)
        sk = step("scalar", lambda e: e.activation(e1b[:, :], s2b[:, :], AF.Exp),
                  waits=[W(sDVE, sk)], inc=sACT)

        def vsum_f(e):
            e.tensor_tensor(sv3, e14[:, :, 0, :], e14[:, :, 1, :], ALU.add)
            e.drain()
            e.tensor_tensor(sv3, sv3, e14[:, :, 2, :], ALU.add)
            e.drain()
            return e.tensor_tensor(sv3, sv3, e14[:, :, 3, :], ALU.add)
        sk = step("vector", vsum_f, waits=[W(sACT, sk)], inc=sDVE)
        sk = step("scalar", lambda e: e.activation(vlb[:, :], svb_[:, :], AF.Ln),
                  waits=[W(sDVE, sk)], inc=sACT)

    sk = step("vector",
              lambda e: e.tensor_tensor(s14, s24, vl3.unsqueeze(2).broadcast_to(
                  (128, 8, 4, 4)), ALU.subtract),
              waits=[W(sACT, sk)], inc=sDVE)
    sk = step("scalar", lambda e: e.activation(hres[:, :], s1b[:, :], AF.Exp),
              waits=[W(sDVE, sk)], inc=sACT)
    hres_done = step("vector",
                     lambda e: e.scalar_tensor_tensor(hres[:, :], hres[:, :],
                                                      gb[:, 0:1], i16b[:, :],
                                                      ALU.mult, ALU.add),
                     waits=[W(sACT, sk), W(sDVE, i16_done)], inc=sDVE)

    hh = {}
    for name, off, buf in (("pre", 16, hpre), ("post", 20, hpost)):
        w_ea = [W(sDVE, lg_all)]
        if "dbgf" in _TAPS:
            w_ea.append(W(sG, sG.n))
        if name == "post":
            w_ea.append(W(sDVE, hh["pre"]))
        ea = step("scalar",
                  lambda e, off=off: e.activation(
                      hwork[:, :].rearrange("p (t j) -> p t j", j=4),
                      lg4[:, :, off:off + 4], AF.Exp),
                  waits=w_ea, inc=sACT)

        def sm_f(e, buf=buf):
            e.tensor_reduce(rsum8[:, :], hwork[:, :].rearrange("p (t j) -> p t j", j=4),
                            AX.X, ALU.add)
            e.drain()
            e.reciprocal(rcp8[:, :], rsum8[:, :])
            e.drain()
            return e.tensor_tensor(buf[:, :].rearrange("p (t j) -> p t j", j=4),
                                   hwork[:, :].rearrange("p (t j) -> p t j", j=4),
                                   rcp8[:, :].rearrange("p t -> p t").unsqueeze(2)
                                   .broadcast_to((128, 8, 4)), ALU.mult)
        hh[name] = step("vector", sm_f, waits=[W(sACT, ea)], inc=sDVE)
    h_all = sDVE.n
    if "dbgf" in _TAPS:
        dma("gpsimd", sG, lambda: dbgf.ap()[:, 0:192], lambda: logits[:, :],
            waits=[W(sDVE, h_all)])
        dma("gpsimd", sG, lambda: dbgf.ap()[:, 192:320], lambda: hres[:, :],
            waits=[W(sDVE, h_all)])
        dma("gpsimd", sG, lambda: dbgf.ap()[:, 320:352], lambda: hpre[:, :],
            waits=[W(sDVE, h_all)])
        dma("gpsimd", sG, lambda: dbgf.ap()[:, 352:384], lambda: hpost[:, :],
            waits=[W(sDVE, h_all)])
        dma("gpsimd", sG, lambda: dbgf.ap()[:, 384:392], lambda: rsum8[:, :],
            waits=[W(sDVE, lg_all)])
        dma("gpsimd", sG, lambda: dbgf.ap()[:, 392:400], lambda: rcp8[:, :],
            waits=[W(sDVE, lg_all)])
        dma("gpsimd", sG, lambda: dbgf.ap()[:, 504:512], lambda: rmsr8[:, :],
            waits=[W(sDVE, lg_all)])

    cur_stage[0] = 'dapp'
    xnm_free2 = {}
    ffn_st = {}
    y_free = {0: 0, 1: 0}
    ffnt_free = 0
    xpre_tr_ev = {0: 0, 1: 0}
    for tt in range(NTT):
        xnm = xnm0 if tt % 2 == 0 else xnm1
        w_x = [W(sDVE, lg_all), W(sPE, sv2_free[NTT - 1])]
        if xnm_free2.get(tt - 2):
            w_x.append(W(sDVE, xnm_free2[tt - 2]))
        ld_xn = dma("sync", sL, lambda xnm=xnm: xnm[:, :],
                    lambda tt=tt: xn_tm_d.ap()[tt * 128:(tt + 1) * 128, :],
                    waits=w_x)

        def xpre_fn(e, xnm=xnm, tt=tt):
            e.tensor_scalar(xpre_f[:, :], xnm[:, 0:1024], hpre[:, tt * 4:tt * 4 + 1],
                            None, ALU.mult)
            for n in (1, 2, 3):
                e.drain()
                e.scalar_tensor_tensor(xpre_f[:, :], xnm[:, n * 1024:(n + 1) * 1024],
                                       hpre[:, tt * 4 + n:tt * 4 + n + 1], xpre_f[:, :],
                                       ALU.mult, ALU.add)
            e.drain()
            return e.tensor_copy(xpre_b[:, :], xpre_f[:, :])
        w_xp = [W(sL, ld_xn), W(sDVE, h_all), W(sDVE, hres_done)]
        xp = step("vector", xpre_fn, waits=w_xp, inc=sDVE)

        for j in range(8):
            ptr = ptr_a if j % 2 == 0 else ptr_b
            w_tr = [W(sDVE, xp)]
            if xpre_tr_ev[j % 2] > 0:
                w_tr.append(W(sDVE, xpre_tr_ev[j % 2]))
            step("tensor",
                 lambda e, j=j, ptr=ptr: e.transpose(
                     ptr, xpre_b[:, j * 128:(j + 1) * 128], eye_sb[:, :]),
                 waits=w_tr, inc=sPE)
            trc = sPE.n
            xpre_tr_ev[j % 2] = step(
                "vector",
                lambda e, j=j, ptr=ptr: e.tensor_copy(
                    xpre_cm[:, j * 128:(j + 1) * 128], ptr),
                waits=[W(sPE, trc)], inc=sDVE)
        xpc = sDVE.n

        y_pe = {}
        for cb in range(2):
            w_y = [W(sDVE, xpc), W(sG, G_wbr)]
            if y_free[cb] > 0:
                w_y.append(W(sDVE, y_free[cb]))

            def ymm(e, cb=cb):
                ins = None
                for ct in range(8):
                    ins = e.matmul(PB[4 + cb][:, :], xpre_cm[:, ct * 128:(ct + 1) * 128],
                                   wbr_sb[:, ct * 1024 + cb * 512: ct * 1024 + cb * 512 + 512],
                                   start=(ct == 0), stop=(ct == 7))
                return ins
            y_pe[cb] = step("tensor", ymm, waits=w_y, inc=sPE)

        last_comb = 0
        for n in range(4):
            for cb in range(2):
                def comb_f(e, n=n, cb=cb, tt=tt, xnm=xnm):
                    sl = slice(cb * 512, cb * 512 + 512)
                    e.tensor_scalar(xpre_f[:, sl], PB[4 + cb][:, :],
                                    hpost[:, tt * 4 + n:tt * 4 + n + 1], None, ALU.mult)
                    ins = None
                    for j in range(4):
                        e.drain()
                        dst = ffnt[:, n * 1024 + cb * 512: n * 1024 + cb * 512 + 512] \
                            if j == 3 else xpre_f[:, sl]
                        ins = e.scalar_tensor_tensor(
                            dst, xnm[:, j * 1024 + cb * 512: j * 1024 + cb * 512 + 512],
                            hres[:, tt * 16 + n * 4 + j: tt * 16 + n * 4 + j + 1],
                            xpre_f[:, sl], ALU.mult, ALU.add)
                    return ins
                w_c = [W(sPE, y_pe[cb])]
                if n == 0 and cb == 0 and ffnt_free > 0:
                    w_c.append(W(sG, ffnt_free))
                last_comb = step("vector", comb_f, waits=w_c, inc=sDVE)
        y_free[0] = last_comb
        y_free[1] = last_comb
        xnm_free2[tt] = last_comb
        ffn_st[tt] = dma("gpsimd", sG,
                         lambda tt=tt: ffn_d.ap()[tt * 128:(tt + 1) * 128, :],
                         lambda: ffnt[:, :],
                         waits=[W(sDVE, last_comb)])
        ffnt_free = ffn_st[tt]
    d_pe_last = sPE.n
    d_dve_last = sDVE.n
    ffn_all_st = sG.n

    cur_stage[0] = 'e'
    def asb_ld_f(e):
        e.wait_ge(sCC.sem, CC_A)
        e.wait_ge(sPE.sem, c_pe_last)
        pid = nc.partition_id(engines=[mybir.EngineType.SP])
        off = e.compute_val(pid * TPC)
        e.dma_start(
            out=a_sb[:, :].rearrange("p (g t) -> p g t", g=G),
            in_=aga_out.ap().rearrange("(g p) t -> p g t", p=128)[:, :, bass.ds(off, TPC)]
        ).then_inc(sL.sem, 16)
    sL.n += 16
    P.on("sync", asb_ld_f)
    L_asb = sL.n

    wo_ld = {}
    wo_free = {}
    for mb in range(2):
        wob = wo_b0 if mb % 2 == 0 else wo_b1
        wo_ld[mb] = dma("gpsimd", sG,
                        lambda wob=wob: wob[:, :].rearrange("p (g n) -> p g n", g=G),
                        lambda mb=mb: wo_in.ap()[:, mb * 512:(mb + 1) * 512]
                            .rearrange("(g p) n -> p g n", p=128),
                        waits=[W(sPE, d_pe_last), W(sDVE, d_dve_last)])

    os_free = {0: 0, 1: 0}
    st_out = {}
    e_mm_done = {}
    for mb in range(8):
        wob = wo_b0 if mb % 2 == 0 else wo_b1
        if mb >= 2:
            wo_ld[mb] = dma("gpsimd", sG,
                            lambda wob=wob: wob[:, :].rearrange("p (g n) -> p g n", g=G),
                            lambda mb=mb: wo_in.ap()[:, mb * 512:(mb + 1) * 512]
                                .rearrange("(g p) n -> p g n", p=128),
                            waits=[W(sPE, e_mm_done[(mb - 2, NTT - 1)])])
        for tt in range(NTT):
            k = mb * NTT + tt
            bank = 6 + k % 2
            xsb = xs0 if k % 2 == 0 else xs1
            fsb = fs0 if k % 2 == 0 else fs1
            osb = os0 if k % 2 == 0 else os1
            w_xs = [W(sPE, c_pe_last)]
            if k >= 2:
                w_xs.append(W(sDVE, os_free[k % 2]))
            ld_xs = dma("sync", sL, lambda xsb=xsb: xsb[:, :],
                        lambda tt=tt, mb=mb: x_in.ap()[tt * 128:(tt + 1) * 128,
                                                       mb * 512:(mb + 1) * 512],
                        waits=w_xs)
            ld_fs = dma("sync", sL, lambda fsb=fsb: fsb[:, :],
                        lambda tt=tt, mb=mb: ffn_d.ap()[tt * 128:(tt + 1) * 128,
                                                        mb * 512:(mb + 1) * 512],
                        waits=[W(sG, ffn_st[tt])])

            def emm(e, tt=tt, bank=bank, wob=wob):
                ins = None
                for g_ in range(G):
                    ins = e.matmul(PB[bank][:, :],
                                   a_sb[:, g_ * TPC + tt * 128: g_ * TPC + (tt + 1) * 128],
                                   wob[:, g_ * 512:(g_ + 1) * 512],
                                   start=(g_ == 0), stop=(g_ == G - 1))
                return ins
            w_mm = [W(sL, L_asb), W(sG, wo_ld[mb])]
            if k == 0:
                w_mm += [W(sPE, d_pe_last), W(sDVE, d_dve_last)]
            if k >= 2:
                w_mm.append(W(sDVE, os_free[k % 2]))
            e_mm_done[(mb, tt)] = step("tensor", emm, waits=w_mm, inc=sPE)

            def add_f(e, osb=osb, xsb=xsb, fsb=fsb, bank=bank):
                e.tensor_tensor(osb[:, :], PB[bank][:, :], xsb[:, :], ALU.add)
                e.drain()
                return e.tensor_tensor(osb[:, :], osb[:, :], fsb[:, :], ALU.add)
            w_add = [W(sPE, e_mm_done[(mb, tt)]), W(sL, ld_xs), W(sL, ld_fs)]
            if k >= 2 and (k - 2) in st_out:
                w_add.append(W(sL, st_out[k - 2]))
            os_free[k % 2] = step("vector", add_f, waits=w_add, inc=sDVE)
            st_out[k] = dma("sync", sL,
                            lambda tt=tt, mb=mb: out_t.ap()[tt * 128:(tt + 1) * 128,
                                                            mb * 512:(mb + 1) * 512],
                            lambda osb=osb: osb[:, :],
                            waits=[W(sDVE, os_free[k % 2])])

    for (tout, srct) in tap_outs:
        dma("gpsimd", sG2, lambda tout=tout: tout.ap(), lambda srct=srct: srct.ap(),
            waits=[W(sL, sL.n), W(sG, sG.n), W(sDVE, sDVE.n), W(sACT, sACT.n)])

    fin_L, fin_G, fin_G2 = sL.n, sG.n, sG2.n
    P.on("sync", lambda e: e.wait_ge(sL.sem, fin_L))
    P.on("gpsimd", lambda e: e.wait_ge(sG.sem, fin_G))
    P.on("gpsimd", lambda e: e.wait_ge(sG2.sem, fin_G2))

    with nc.Block() as blk:
        P.flush(blk)
    es.close()
    return nc


_cache = {}


def kernel(**inputs):
    bf = ml_dtypes.bfloat16
    x = np.asarray(inputs["L0_attn_hidden_states"])
    wq = np.asarray(inputs["L0_attn_wq"])
    wk = np.asarray(inputs["L0_attn_wk"])
    wv = np.asarray(inputs["L0_attn_wv"])
    wo = np.ascontiguousarray(np.asarray(inputs["L0_attn_wo"]))
    wbr = np.ascontiguousarray(np.asarray(inputs["L0_ffn_W_branch"]))
    wall = np.ascontiguousarray(np.concatenate([
        np.asarray(inputs["L0_ffn_W_res"]),
        np.asarray(inputs["L0_ffn_W_pre"]),
        np.asarray(inputs["L0_ffn_W_post"])], axis=0))
    bases = np.ascontiguousarray(np.concatenate([
        np.asarray(inputs["L0_ffn_H_res_base"]).reshape(-1),
        np.asarray(inputs["L0_ffn_H_pre_base"]).reshape(-1),
        np.asarray(inputs["L0_ffn_H_post_base"]).reshape(-1)])[None, :]).astype(np.float32)
    gate = np.ascontiguousarray(np.asarray(inputs["L0_ffn_gate"]).reshape(1, 1)).astype(np.float32)

    eye = np.eye(128, dtype=np.float32).astype(bf)
    ones = np.ones((128, 128), dtype=np.float32).astype(bf)
    iota_d = (np.arange(128, dtype=np.int64) % 64).astype(np.float32)[None, :]
    iota_t = np.arange(S, dtype=np.int64).astype(np.float32)[None, :]
    k_idx = np.arange(128)[:, None]
    j_idx = np.arange(1024)[None, :]
    cmask = np.ascontiguousarray(((j_idx - 384) >= k_idx).astype(np.float32).astype(bf))

    if "nc" not in _cache:
        _cache["nc"] = build_program()
    nc = _cache["nc"]

    in_maps = []
    for c in range(NCO):
        in_maps.append({
            "x_slice": np.ascontiguousarray(x[0, c * TPC:(c + 1) * TPC, :]),
            "wq_s": np.ascontiguousarray(wq[:, c * 512:(c + 1) * 512]),
            "wk_s": np.ascontiguousarray(wk[:, c * 512:(c + 1) * 512]),
            "wv_s": np.ascontiguousarray(wv[:, c * 512:(c + 1) * 512]),
            "wo": wo, "wbr": wbr, "wall": wall, "bases": bases, "gate": gate,
            "eye": eye, "onesb": ones, "iota_d": iota_d, "iota_t": iota_t,
            "cmask": cmask,
        })
    res = run_bass_kernel_spmd(nc, in_maps, core_ids=list(range(NCO)))
    if _TAPS:
        _cache["taps"] = res
    out = np.concatenate([res.results[c]["out"] for c in range(NCO)], axis=0)
    return out.reshape(1, S, D)


# version 05
# speedup vs baseline: 1.0868x; 1.0868x over previous
import numpy as np
import ml_dtypes
from contextlib import ExitStack

import concourse.bass as bass
import concourse.mybir as mybir
from concourse.bass_utils import run_bass_kernel_spmd

F32 = mybir.dt.float32
BF16 = mybir.dt.bfloat16
I32 = mybir.dt.int32
AF = mybir.ActivationFunctionType
ALU = mybir.AluOpType
AX = mybir.AxisListType

NCO = 8
S = 8192
D = 4096
TPC = S // NCO
NTT = TPC // 128
G = D // 128
NH = 4
TC = S // 512
VT = S // 128
QB = 16
SCALE = float(1.0 / np.sqrt(128.0))
EXP_BIAS = -10.0
LN_EPS = 1e-5
RMS_EPS = 1.1920929e-07
SINK_ITERS = 50

_TAPS = ()
_SERIAL = False
_SERIAL_STAGE = None
_SERIAL_MASK = 'PADLG'

ENGS = ("sync", "gpsimd", "tensor", "vector", "scalar")


class Ctr:
    def __init__(self, es, nc, name):
        self.sem = es.enter_context(nc.semaphore(name))
        self.n = 0


class Prog:

    def __init__(self):
        self.q = {e: [] for e in ENGS}

    def on(self, eng, fn):
        self.q[eng].append(fn)

    def flush(self, blk):
        for eng in ENGS:
            fns = self.q[eng]
            getattr(blk, eng)(lambda e, fns=fns: [f(e) for f in fns])


def build_program():
    nc = bass.Bass("TRN2", target_bir_lowering=False)
    es = ExitStack()
    P = Prog()

    x_in = nc.dram_tensor("x_slice", [TPC, D], F32, kind="ExternalInput")
    wq_in = nc.dram_tensor("wq_s", [D, 512], F32, kind="ExternalInput")
    wk_in = nc.dram_tensor("wk_s", [D, 512], F32, kind="ExternalInput")
    wv_in = nc.dram_tensor("wv_s", [D, 512], F32, kind="ExternalInput")
    wo_in = nc.dram_tensor("wo", [D, D], F32, kind="ExternalInput")
    wbr_in = nc.dram_tensor("wbr", [1024, 1024], F32, kind="ExternalInput")
    wall_in = nc.dram_tensor("wall", [24, D], F32, kind="ExternalInput")
    bases_in = nc.dram_tensor("bases", [1, 24], F32, kind="ExternalInput")
    gate_in = nc.dram_tensor("gate", [1, 1], F32, kind="ExternalInput")
    eye_in = nc.dram_tensor("eye", [128, 128], BF16, kind="ExternalInput")
    ones_in = nc.dram_tensor("onesb", [128, 128], BF16, kind="ExternalInput")
    iota_d_in = nc.dram_tensor("iota_d", [1, 128], F32, kind="ExternalInput")
    iota_t_in = nc.dram_tensor("iota_t", [1, S], F32, kind="ExternalInput")
    cmask_in = nc.dram_tensor("cmask", [128, 1024], BF16, kind="ExternalInput")

    out_t = nc.dram_tensor("out", [TPC, D], F32, kind="ExternalOutput")

    agx_in = nc.dram_tensor("agx_in", [D, TPC], BF16)
    agx_out = nc.dram_tensor("agx_out", [NCO * D, TPC], BF16)
    xn_cm_d = nc.dram_tensor("xn_cm_d", [D, TPC], BF16)
    xn_tm_d = nc.dram_tensor("xn_tm_d", [TPC, D], BF16)
    q_dram = nc.dram_tensor("q_dram", [512, S], BF16)
    k_dram = nc.dram_tensor("k_dram", [512, S], BF16)
    aga_in = nc.dram_tensor("aga_in", [512, S], BF16)
    agh_out = [nc.dram_tensor("agh_out%d" % h, [NCO * 128, S], BF16) for h in range(NH)]
    ffn_d = nc.dram_tensor("ffn_d", [TPC, D], BF16)
    dbgf = nc.dram_tensor("dbgf", [128, 8192], F32)
    _tapmap = {"agx_out": agx_out, "xn_tm_d": xn_tm_d, "q_dram": q_dram,
               "k_dram": k_dram, "aga_in": aga_in, "aga_out": agh_out[0],
               "ffn_d": ffn_d, "xn_cm_d": xn_cm_d, "dbgf": dbgf}
    tap_outs = []
    for tn in _TAPS:
        src = _tapmap[tn]
        tap_outs.append((nc.dram_tensor("tap_" + tn, list(src.shape), BF16,
                                        kind="ExternalOutput") if src is not dbgf else
                         nc.dram_tensor("tap_" + tn, list(src.shape), F32,
                                        kind="ExternalOutput"), src))

    SB_BASE = 16512

    def A(name, off, shape, dt):
        return nc.alloc_sbuf_tensor_at(name, shape, dt, offset=SB_BASE + off)

    xn_cm = A("xn_cm", 0, [128, G * TPC], BF16)
    v_sb = A("v_sb", 0, [128, VT * 512], BF16)
    a_sb = A("a_sb", 0, [128, G * TPC], BF16)

    wq_sb = A("wq_sb", 65536, [128, G * 512], BF16)
    wk_sb = A("wk_sb", 98304, [128, G * 512], BF16)
    wo_b0 = A("wo_b0", 65536, [128, G * 512], BF16)
    wo_b1 = A("wo_b1", 98304, [128, G * 512], BF16)
    wbr_sb = A("wbr_sb", 65536, [128, 8 * 1024], BF16)
    xnm0 = A("xnm0", 81920, [128, D], BF16)
    xnm1 = A("xnm1", 90112, [128, D], BF16)
    wall_sb = A("wall_sb", 98304, [24, D], BF16)
    xpre_f = A("xpre_f", 106496, [128, 1024], F32)
    xpre_b = A("xpre_b", 110592, [128, 1024], BF16)
    xpre_cm = A("xpre_cm", 112640, [128, 1024], BF16)
    dso = 114688
    logits = A("logits", dso, [128, 8 * 24], F32); dso += 768
    zv = A("zv", dso, [128, 128], F32); dso += 512
    s1b = A("s1b", dso, [128, 128], F32); dso += 512
    s2b = A("s2b", dso, [128, 128], F32); dso += 512
    e1b = A("e1b", dso, [128, 128], F32); dso += 512
    sub_ = A("sub_", dso, [128, 32], F32); dso += 128
    svb_ = A("svb_", dso, [128, 32], F32); dso += 128
    ulb = A("ulb", dso, [128, 32], F32); dso += 128
    vlb = A("vlb", dso, [128, 32], F32); dso += 128
    hres = A("hres", dso, [128, 128], F32); dso += 512
    i16b = A("i16b", dso, [128, 128], F32); dso += 512
    hpre = A("hpre", dso, [128, 32], F32); dso += 128
    hpost = A("hpost", dso, [128, 32], F32); dso += 128
    hwork = A("hwork", dso, [128, 32], F32); dso += 128
    rsum8 = A("rsum8", dso, [128, 8], F32); dso += 32
    rcp8 = A("rcp8", dso, [128, 8], F32); dso += 32
    wallT = A("wallT", dso, [128, G * 24], BF16); dso += 1536
    basesb = A("basesb", dso, [128, 24], F32); dso += 96
    gb = A("gb", dso, [128, 2], F32); dso += 32
    assert dso <= 131072, dso

    cos_sb = A("cos_sb", 131072, [128, S], BF16)
    sin_sb = A("sin_sb", 147456, [128, S], BF16)
    sv0 = A("sv0", 131072, [128, G * 128], BF16)
    sv1 = A("sv1", 139264, [128, G * 128], BF16)
    kh0 = A("kh0", 131072, [128, S], BF16)
    kh1 = A("kh1", 147456, [128, S], BF16)
    xs0 = A("xs0", 131072, [128, 512], F32)
    xs1 = A("xs1", 133120, [128, 512], F32)
    fs0 = A("fs0", 135168, [128, 512], BF16)
    fs1 = A("fs1", 136192, [128, 512], BF16)
    os0 = A("os0", 137216, [128, 512], F32)
    os1 = A("os1", 139264, [128, 512], F32)
    sv3 = A("sv3", 81920, [128, G * 128], BF16)

    xb0 = A("xb0", 163840, [128, 2048], F32)
    xb1 = A("xb1", 172032, [128, 2048], F32)
    xnt0 = A("xnt0", 180224, [128, D], BF16)
    xnt1 = A("xnt1", 188416, [128, D], BF16)
    mv0 = A("mv0", 163840, [128, 16 * 512], BF16)
    mv1 = A("mv1", 180224, [128, 16 * 512], BF16)
    wv_sb = A("wv_sb", 163840, [128, G * 512], BF16)
    co = 163840
    pb3 = [A("pb3_%d" % i, co + 2048 * i, [128, 1024], BF16) for i in range(3)]; co += 6144
    qt0 = A("qt0", co, [128, 512], BF16); co += 1024
    qt1 = A("qt1", co, [128, 512], BF16); co += 1024
    at0 = A("at0", co, [128, 512], BF16); co += 1024
    at1 = A("at1", co, [128, 512], BF16); co += 1024
    rec0 = A("rec0", co, [128, 512], F32); co += 2048
    rec1 = A("rec1", co, [128, 512], F32); co += 2048
    cmask_sb = A("cmask_sb", co, [128, 1024], BF16); co += 2048
    sv2 = A("sv2", co, [128, G * 128], BF16); co += 8192
    ffnt = A("ffnt", co, [128, D], BF16); co += 8192
    assert co <= 196608, co

    so = 196608
    ones_sb = A("ones_sb", so, [128, 128], BF16); so += 256
    eye_sb = A("eye_sb", so, [128, 128], BF16); so += 256
    consts = A("consts", so, [128, 8], F32); so += 32
    stats = A("stats", so, [128, 8], F32); so += 32
    onesf = A("onesf", so, [1, 128], F32); so += 512
    invf = A("invf", so, [1, 128], F32); so += 512
    iotc = A("iotc", so, [1, 512], F32); so += 2048
    frq0 = A("frq0", so, [128, 512], F32); so += 2048
    frq1 = A("frq1", so, [128, 512], F32); so += 2048
    kint = A("kint", so, [128, 512], I32); so += 2048
    prt = A("prt", so, [128, 512], F32); so += 2048
    rt1 = A("rt1", so, [128, 512], F32); so += 2048
    qkb0 = A("qkb0", so, [128, 512], BF16); so += 1024
    qkb1 = A("qkb1", so, [128, 512], BF16); so += 1024
    gatesb = A("gatesb", so, [1, 4], F32); so += 32
    gcat = A("gcat", so, [1, 2], F32); so += 32
    basesr = A("basesr", so, [1, 24], F32); so += 96
    sxn = A("sxn", so, [128, 2], F32); so += 32
    rmsr8 = A("rmsr8", so, [128, 8], F32); so += 32
    assert so <= 212800, so

    PPAIR = [es.enter_context(nc.psum_tensor("pp%d" % i, [128, 1024], F32)) for i in range(3)]
    PO = es.enter_context(nc.psum_tensor("po", [128, 512], F32))
    PR = es.enter_context(nc.psum_tensor("pr", [128, 512], F32))
    PB = [PPAIR[0][:, 0:512], PPAIR[0][:, 512:1024],
          PPAIR[1][:, 0:512], PPAIR[1][:, 512:1024],
          PPAIR[2][:, 0:512], PPAIR[2][:, 512:1024],
          PO[:, :], PR[:, :]]
    ptr_a = PPAIR[0][:, :].bitcast(BF16)[:, 0:128]
    ptr_b = PPAIR[0][:, :].bitcast(BF16)[:, 1024:1152]

    sL = Ctr(es, nc, "sL")
    sG = Ctr(es, nc, "sG")
    sCC = Ctr(es, nc, "sCC")
    sPE = Ctr(es, nc, "sPE")
    sACT = Ctr(es, nc, "sACT")
    sDVE = Ctr(es, nc, "sDVE")
    sPL = Ctr(es, nc, "sPL")
    sL2 = Ctr(es, nc, "sL2")
    sG2 = Ctr(es, nc, "sG2")

    def dma(queue, ctr, outf, inf, waits=()):
        waits = list(waits)
        if _SERIAL or cur_stage[0] == 'dlog' or (_SERIAL_STAGE is not None and cur_stage[0] == _SERIAL_STAGE):
            mm = _SERIAL_MASK
            if 'P' in mm: waits.append((sPE.sem, sPE.n))
            if 'A' in mm: waits.append((sACT.sem, sACT.n))
            if 'D' in mm: waits.append((sDVE.sem, sDVE.n))
        ctr.n += 16
        def f(e, outf=outf, inf=inf, waits=tuple(waits), sem=ctr.sem):
            for (s, v) in waits:
                if v > 0:
                    e.wait_ge(s, v)
            e.dma_start(out=outf(), in_=inf()).then_inc(sem, 16)
        P.on(queue, f)
        return ctr.n

    cur_stage = [None]

    def step(eng, fn, waits=(), inc=None):
        waits = list(waits)
        if _SERIAL or cur_stage[0] == 'dlog' or (_SERIAL_STAGE is not None and cur_stage[0] == _SERIAL_STAGE):
            mm = _SERIAL_MASK
            if 'P' in mm: waits.append((sPE.sem, sPE.n))
            if 'A' in mm: waits.append((sACT.sem, sACT.n))
            if 'D' in mm: waits.append((sDVE.sem, sDVE.n))
            if 'L' in mm: waits.append((sL.sem, sL.n))
            if 'G' in mm: waits.append((sG.sem, sG.n))
        if inc is not None:
            inc.n += 1
        def f(e, fn=fn, waits=tuple(waits), inc=inc):
            for (s, v) in waits:
                if v > 0:
                    e.wait_ge(s, v)
            ins = fn(e)
            if inc is not None:
                ins.then_inc(inc.sem, 1)
        P.on(eng, f)
        return inc.n if inc is not None else 0

    W = lambda ctr, v: (ctr.sem, v)

    dma("sync", sL, lambda: eye_sb[:, :], lambda: eye_in.ap())
    L_const = dma("sync", sL, lambda: ones_sb[:, :], lambda: ones_in.ap())
    dma("sync", sL2, lambda: basesr[:, :], lambda: bases_in.ap())
    dma("sync", sL2, lambda: gatesb[0:1, 0:1], lambda: gate_in.ap())
    L2_const = dma("sync", sL2, lambda: invf[:, :], lambda: iota_d_in.ap())

    def memsets(e):
        e.memset(consts[:, 0:1], EXP_BIAS)
        e.memset(consts[:, 1:2], SCALE)
        e.memset(consts[:, 2:3], LN_EPS)
        e.memset(consts[:, 3:4], -0.5)
        e.memset(consts[:, 4:5], RMS_EPS)
        e.memset(consts[:, 5:6], -1.0)
        return e.memset(onesf[:, :], 1.0)
    P_const = step("gpsimd", memsets, inc=sPL)

    A_invf = step("scalar",
                  lambda e: e.activation(invf[:, :], invf[:, :], AF.Exp,
                                         scale=-float(np.log(10000.0) / 64.0)),
                  waits=[W(sL2, L2_const), W(sPL, P_const)], inc=sACT)

    G_wq = dma("gpsimd", sG,
               lambda: wq_sb[:, :].rearrange("p (g n) -> p g n", g=G),
               lambda: wq_in.ap().rearrange("(g p) n -> p g n", p=128))
    G_wqk = dma("gpsimd", sG,
                lambda: wk_sb[:, :].rearrange("p (g n) -> p g n", g=G),
                lambda: wk_in.ap().rearrange("(g p) n -> p g n", p=128))

    ln_xn_done = [0] * NTT
    rmsr_done = [0] * NTT
    ln_ev_done = [0] * NTT
    ln_st_done = [0] * NTT
    bank_ev = {}
    x_ld = {}

    for tt in range(NTT):
        for half in range(2):
            xb = xb0 if half == 0 else xb1
            w = [W(sDVE, ln_xn_done[tt - 1])] if tt > 0 else []
            x_ld[(tt, half)] = dma(
                "sync", sL,
                lambda xb=xb: xb[:, :],
                lambda tt=tt, half=half: x_in.ap()[tt * 128:(tt + 1) * 128,
                                                  half * 2048:(half + 1) * 2048],
                waits=w)

        xnt = xnt0 if tt % 2 == 0 else xnt1
        def stats_f(e, tt=tt, xnt=xnt):
            for half, xb in ((0, xb0), (1, xb1)):
                e.scalar_tensor_tensor(xnt[:, half * 2048:(half + 1) * 2048],
                                       xb[:, :], 1.0, xb[:, :], ALU.bypass, ALU.mult,
                                       accum_out=stats[:, half:half + 1])
                e.tensor_reduce(stats[:, 2 + half:3 + half], xb[:, :], AX.X, ALU.add)
            e.drain()
            e.tensor_tensor(stats[:, 2:3], stats[:, 2:3], stats[:, 3:4], ALU.add)
            e.tensor_tensor(stats[:, 0:1], stats[:, 0:1], stats[:, 1:2], ALU.add)
            e.drain()
            e.tensor_scalar(stats[:, 2:3], stats[:, 2:3], 1.0 / D, None, ALU.mult)
            e.drain()
            e.tensor_tensor(stats[:, 3:4], stats[:, 2:3], stats[:, 2:3], ALU.mult)
            e.drain()
            return e.tensor_scalar(stats[:, 1:2], stats[:, 0:1], 1.0 / D,
                                   stats[:, 3:4], ALU.mult, ALU.subtract)
        w_st = [W(sL, x_ld[(tt, 0)]), W(sL, x_ld[(tt, 1)]), W(sPL, P_const)]
        if tt > 0:
            w_st.append(W(sACT, rmsr_done[tt - 1]))
        if tt > 1:
            w_st += [W(sDVE, ln_ev_done[tt - 2]), W(sG, ln_st_done[tt - 2])]
        var_done = step("vector", stats_f, waits=w_st, inc=sDVE)

        def rstd_f(e):
            e.activation(stats[:, 5:6], stats[:, 1:2], AF.Ln, bias=consts[:, 2:3])
            e.drain()
            return e.activation(stats[:, 4:5], stats[:, 5:6], AF.Exp, scale=consts[:, 3:4])
        rstd_done = step("scalar", rstd_f, waits=[W(sDVE, var_done)], inc=sACT)

        def xn_f(e, xnt=xnt):
            e.tensor_scalar(xnt[:, 0:2048], xb0[:, :], stats[:, 2:3], stats[:, 4:5],
                            ALU.subtract, ALU.mult)
            return e.tensor_scalar(xnt[:, 2048:4096], xb1[:, :], stats[:, 2:3],
                                   stats[:, 4:5], ALU.subtract, ALU.mult)
        ln_xn_done[tt] = step("vector", xn_f, waits=[W(sACT, rstd_done)], inc=sDVE)

        def msq_f(e, tt=tt):
            e.tensor_scalar(stats[:, 0:1], stats[:, 1:2], LN_EPS, None, ALU.add)
            e.drain()
            e.reciprocal(stats[:, 3:4], stats[:, 0:1])
            e.drain()
            return e.tensor_tensor(stats[:, 0:1], stats[:, 1:2], stats[:, 3:4], ALU.mult)
        msq_done = step("vector", msq_f, waits=[W(sACT, rstd_done)], inc=sDVE)

        def rmsr_f(e, tt=tt):
            e.activation(stats[:, 3:4], stats[:, 0:1], AF.Ln, bias=consts[:, 4:5])
            e.drain()
            return e.activation(rmsr8[:, tt:tt + 1], stats[:, 3:4], AF.Exp,
                                scale=consts[:, 3:4])
        rmsr_done[tt] = step("scalar", rmsr_f, waits=[W(sDVE, msq_done)], inc=sACT)

        ln_st_done[tt] = dma("gpsimd", sG,
                             lambda tt=tt: xn_tm_d.ap()[tt * 128:(tt + 1) * 128, :],
                             lambda xnt=xnt: xnt[:, :],
                             waits=[W(sDVE, ln_xn_done[tt])])

        for g_ in range(G):
            bank = g_ % 2
            ptr = ptr_a if bank == 0 else ptr_b
            w_tr = [W(sDVE, ln_xn_done[tt])]
            if bank_ev.get(bank, 0) > 0:
                w_tr.append(W(sDVE, bank_ev[bank]))
            step("tensor",
                 lambda e, xnt=xnt, g_=g_, ptr=ptr: e.transpose(
                     ptr, xnt[:, g_ * 128:(g_ + 1) * 128], eye_sb[:, :]),
                 waits=w_tr, inc=sPE)
            tr_cnt = sPE.n
            bank_ev[bank] = step(
                "vector",
                lambda e, tt=tt, g_=g_, ptr=ptr: e.tensor_copy(
                    xn_cm[:, g_ * TPC + tt * 128: g_ * TPC + (tt + 1) * 128], ptr),
                waits=[W(sPE, tr_cnt)], inc=sDVE)
        ln_ev_done[tt] = sDVE.n

    ln_all_ev = sDVE.n
    G_agx = dma("gpsimd", sG,
                lambda: agx_in.ap().rearrange("(g p) t -> p g t", p=128),
                lambda: xn_cm[:, :].rearrange("p (g t) -> p g t", g=G),
                waits=[W(sDVE, ln_all_ev)])
    G_xcm = dma("gpsimd", sG,
                lambda: xn_cm_d.ap().rearrange("(g p) t -> p g t", p=128),
                lambda: xn_cm[:, :].rearrange("p (g t) -> p g t", g=G))

    def agx_f(e):
        e.wait_ge(sG.sem, G_xcm)
        return e.collective_compute("AllGather", ALU.bypass,
                                    replica_groups=[list(range(NCO))],
                                    ins=[agx_in.ap()], outs=[agx_out.ap()])
    sCC.n += 1
    P.on("gpsimd", lambda e: agx_f(e).then_inc(sCC.sem, 1))
    CC_X = sCC.n

    tab_dve = []
    for tcx in range(TC):
        w = [W(sDVE, tab_dve[tcx - 1][1])] if tcx else []
        iot_ld = dma("sync", sL2, lambda: iotc[:, :],
                     lambda tcx=tcx: iota_t_in.ap()[:, tcx * 512:(tcx + 1) * 512],
                     waits=w)
        w_mm = [W(sL2, iot_ld), W(sACT, A_invf)]
        if tcx:
            w_mm.append(W(sDVE, tab_dve[tcx - 1][1]))
        frq_pe = step("tensor",
                      lambda e: e.matmul(PB[0], invf[:, :], iotc[:, :],
                                         start=True, stop=True),
                      waits=w_mm, inc=sPE)

        def redu_f(e):
            e.tensor_scalar(kint[:, :], PB[0], float(1.0 / (2 * np.pi)), None, ALU.mult)
            e.drain()
            e.tensor_copy(frq0[:, :], kint[:, :])
            e.drain()
            e.scalar_tensor_tensor(frq0[:, :], frq0[:, :], -float(2 * np.pi),
                                   PB[0], ALU.mult, ALU.add)
            e.tensor_scalar(frq1[:, :], PB[0], float(np.pi / 2), None, ALU.add)
            e.drain()
            e.tensor_scalar(kint[:, :], frq1[:, :], float(1.0 / (2 * np.pi)), None, ALU.mult)
            e.drain()
            e.tensor_copy(rt1[:, :], kint[:, :])
            e.drain()
            return e.scalar_tensor_tensor(frq1[:, :], rt1[:, :], -float(2 * np.pi),
                                          frq1[:, :], ALU.mult, ALU.add)
        rr = step("vector", redu_f, waits=[W(sPE, frq_pe)], inc=sDVE)
        tab_dve.append((rr, rr))

        def sin_f(e, tcx=tcx):
            e.activation(sin_sb[:, tcx * 512:(tcx + 1) * 512], frq0[:, :], AF.Sin)
            return e.activation(cos_sb[:, tcx * 512:(tcx + 1) * 512], frq1[:, :], AF.Sin)
        sa = step("scalar", sin_f, waits=[W(sDVE, rr)], inc=sACT)
        tab_dve[tcx] = (rr, rr, sa)
    tab_act = sACT.n
    tab_done = step("vector",
                    lambda e: e.tensor_scalar(sin_sb[0:64, :], sin_sb[0:64, :],
                                              -1.0, None, ALU.mult),
                    waits=[W(sACT, tab_act)], inc=sDVE)

    WQK = (wq_sb, wk_sb)
    mov_ld = {}
    mov_free = {}
    rope_m1 = {}
    rope_out = {}
    qk_stores = {}
    for tcx in range(TC):
        for half in range(2):
            idx = tcx * 2 + half
            mv = mv0 if half == 0 else mv1
            w = [W(sCC, CC_X)]
            if idx >= 2:
                w.append(W(sPE, mov_free[idx - 2]))
            r, sh = tcx // 2, (tcx % 2) * 512
            mov_ld[idx] = dma(
                "sync", sL,
                lambda mv=mv: mv[:, :].rearrange("p (g t) -> p g t", g=16),
                lambda r=r, half=half, sh=sh: agx_out.ap()
                    .rearrange("(r g p) t -> r p g t", g=G, p=128)[r]
                    [:, half * 16:(half + 1) * 16, sh:sh + 512],
                waits=w)

        for half in range(2):
            idx = tcx * 2 + half
            mv = mv0 if half == 0 else mv1
            for gg in range(16):
                g_ = half * 16 + gg
                for j in range(8):
                    w_mm = []
                    if g_ == 0 and j == 0:
                        w_mm = [W(sL, mov_ld[idx]), W(sG, G_wqk)]
                        if tcx == 0:
                            w_mm += [W(sDVE, tab_done), W(sDVE, ln_all_ev)]
                    elif gg == 0 and j == 0:
                        w_mm = [W(sL, mov_ld[idx])]
                    if g_ == 0 and tcx > 0:
                        w_mm.append(W(sDVE, rope_m1[(tcx - 1, j)]))
                    wsb = WQK[j // 4]
                    jj = j % 4
                    last = (g_ == G - 1)
                    step("tensor",
                         lambda e, wsb=wsb, g_=g_, jj=jj, j=j, mv=mv, gg=gg, first=(g_ == 0), last=last:
                         e.matmul(PB[j],
                                  wsb[:, g_ * 512 + jj * 128: g_ * 512 + (jj + 1) * 128],
                                  mv[:, gg * 512:(gg + 1) * 512],
                                  start=first, stop=last),
                         waits=w_mm, inc=(sPE if (last or gg == 15) else None))
            mov_free[idx] = sPE.n
        qk_pe_last = sPE.n

        for j in range(8):
            qkb = qkb0 if j % 2 == 0 else qkb1
            w_r = [W(sPE, qk_pe_last - 8 + j + 1)]
            prev = qk_stores.get((tcx, j - 2)) or (qk_stores.get((tcx - 1, j + 6)) if tcx else None)
            if prev:
                w_r.append(W(sG, prev))

            def rope_f1(e, j=j, tcx=tcx):
                e.tensor_copy(prt[0:64, :], PB[j][64:128, :])
                e.tensor_copy(prt[64:128, :], PB[j][0:64, :])
                e.drain()
                return e.tensor_tensor(rt1[:, :], PB[j],
                                       cos_sb[:, tcx * 512:(tcx + 1) * 512], ALU.mult)
            rope_m1[(tcx, j)] = step("vector", rope_f1, waits=w_r, inc=sDVE)

            def rope_f2(e, j=j, tcx=tcx, qkb=qkb):
                e.tensor_tensor(frq0[:, :], prt[:, :],
                                sin_sb[:, tcx * 512:(tcx + 1) * 512], ALU.mult)
                e.drain()
                return e.tensor_tensor(qkb[:, :], rt1[:, :], frq0[:, :], ALU.add)
            rope_out[(tcx, j)] = step("vector", rope_f2, inc=sDVE)

            dest = q_dram if j < 4 else k_dram
            qk_stores[(tcx, j)] = dma(
                "gpsimd", sG,
                lambda dest=dest, j=j, tcx=tcx: dest.ap()[(j % 4) * 128:(j % 4 + 1) * 128,
                                                          tcx * 512:(tcx + 1) * 512],
                lambda qkb=qkb: qkb[:, :],
                waits=[W(sDVE, rope_out[(tcx, j)])])
    qk_all_pe = sPE.n
    rope_all = sDVE.n
    qk_all_st = sG.n

    G_wv = dma("gpsimd", sG,
               lambda: wv_sb[:, :].rearrange("p (g n) -> p g n", g=G),
               lambda: wv_in.ap().rearrange("(g p) n -> p g n", p=128),
               waits=[W(sPE, qk_all_pe)])

    sv_ld = {}
    v_pe = {}
    v_ev = {}
    for vt in range(VT):
        sv = sv0 if vt % 2 == 0 else sv1
        w = []
        if vt == 0:
            w = [W(sPE, qk_all_pe), W(sDVE, rope_all)]
        if vt >= 2:
            w.append(W(sPE, v_pe[vt - 2]))
        r, tt_ = vt // 8, vt % 8
        sv_ld[vt] = dma(
            "sync", sL,
            lambda sv=sv: sv[:, :].rearrange("p (g t) -> p g t", g=G),
            lambda r=r, tt_=tt_: agx_out.ap()
                .rearrange("(r g p) t -> r p g t", g=G, p=128)[r]
                [:, :, tt_ * 128:(tt_ + 1) * 128],
            waits=w)

        pb = PPAIR[0][:, 0:512] if vt % 2 == 0 else PPAIR[0][:, 512:1024]
        w_mm = [W(sL, sv_ld[vt]), W(sG, G_wv)]
        if vt == 0:
            w_mm += [W(sG, G_xcm), W(sDVE, rope_all)]
        if vt >= 2:
            w_mm.append(W(sDVE, v_ev[vt - 2]))

        def vmm_f(e, sv=sv, pb=pb):
            ins = None
            for g_ in range(G):
                ins = e.matmul(pb[:, :], sv[:, g_ * 128:(g_ + 1) * 128],
                               wv_sb[:, g_ * 512:(g_ + 1) * 512],
                               start=(g_ == 0), stop=(g_ == G - 1))
            return ins
        v_pe[vt] = step("tensor", vmm_f, waits=w_mm, inc=sPE)
        v_ev[vt] = step("vector",
                        lambda e, vt=vt, pb=pb: e.tensor_copy(
                            v_sb[:, vt * 512:(vt + 1) * 512], pb[:, :]),
                        waits=[W(sPE, v_pe[vt])], inc=sDVE)
    v_all_pe = sPE.n
    v_all = sDVE.n

    G_wbr = dma("gpsimd", sG,
                lambda: wbr_sb[:, :].rearrange("p (g n) -> p g n", g=8),
                lambda: wbr_in.ap().rearrange("(g p) n -> p g n", p=128),
                waits=[W(sPE, qk_all_pe)])
    G_wall = dma("gpsimd", sG2, lambda: wall_sb[:, :], lambda: wall_in.ap())
    G_ffnw = sG.n

    kh_ld = {}
    L_cmask = dma("sync", sL, lambda: cmask_sb[:, :], lambda: cmask_in.ap(),
                  waits=[W(sPE, v_all_pe), W(sDVE, rope_all)])
    kh_ld[0] = dma("sync", sL, lambda: kh0[:, :], lambda: k_dram.ap()[0:128, :],
                   waits=[W(sG, qk_all_st)])

    exp_on_bank = {}
    pv_on_p = {}
    qt_ld = {}
    qt_free = {}
    kh_free = {}
    at_store = {}
    at_cons = 0
    po_free = 0
    rec_free = 0
    c_entry = 0
    it = 0
    for h in range(NH):
        kh = kh0 if h % 2 == 0 else kh1
        if h + 1 < NH:
            khn = kh1 if (h + 1) % 2 == 1 else kh0
            w = [W(sPE, kh_free[h - 1])] if h >= 1 else [W(sPE, v_all_pe), W(sDVE, rope_all)]
            kh_ld[h + 1] = dma("sync", sL, lambda khn=khn: khn[:, :],
                               lambda h=h: k_dram.ap()[(h + 1) * 128:(h + 2) * 128, :],
                               waits=w)
        for qb in range(QB):
            qt = qt0 if it % 2 == 0 else qt1
            at = at0 if it % 2 == 0 else at1
            nkc = 4 * (qb + 1)
            w = [W(sG, qk_all_st)] if it == 0 else []
            if it >= 2:
                w.append(W(sPE, qt_free[it - 2]))
            qt_ld[it] = dma("sync", sL, lambda qt=qt: qt[:, :],
                            lambda h=h, qb=qb: q_dram.ap()[h * 128:(h + 1) * 128,
                                                           qb * 512:(qb + 1) * 512],
                            waits=w)
            mm1_last = 0
            exp_cnt = {}
            p_rdy = {}
            npairs = nkc // 2

            def emit_pv_pair(kp):
                for half in (0, 1):
                    kc = 2 * kp + half
                    pbs = pb3[kp % 3][:, half * 512:half * 512 + 512]
                    eng, val = p_rdy[kc]
                    w2 = [(eng, val)]
                    if kc == 0 and po_free > 0:
                        w2.append(W(sDVE, po_free))
                    step("tensor",
                         lambda e, kc=kc, h=h, pbs=pbs, first=(kc == 0), last=(kc == nkc - 1):
                         e.matmul(PO[:, :],
                                  v_sb[:, kc * 512 + h * 128: kc * 512 + (h + 1) * 128],
                                  pbs, start=first, stop=last),
                         waits=w2)
                    w3 = []
                    if kc == 0 and rec_free > 0:
                        w3 = [W(sACT, rec_free)]
                    cnt = step(
                        "tensor",
                        lambda e, pbs=pbs, first=(kc == 0), last=(kc == nkc - 1):
                        e.matmul(PR[:, :], ones_sb[:, :], pbs,
                                 start=first, stop=last),
                        waits=w3, inc=(sPE if half == 1 else None))
                    if half == 1:
                        pv_on_p[kp % 3] = cnt

            for kp in range(npairs):
                rb = kp % 3
                SP = PPAIR[rb]
                PPB = pb3[rb]
                w_mm1 = []
                if kp == 0:
                    w_mm1 = [W(sL, qt_ld[it]), W(sL, kh_ld[h])]
                    if it == 0:
                        w_mm1 += [W(sDVE, v_all), W(sDVE, rope_all)]
                if exp_on_bank.get(rb, 0) > 0:
                    w_mm1.append(W(sACT, exp_on_bank[rb]))
                step("tensor",
                     lambda e, SP=SP, kh=kh, kc=2 * kp, qt=qt:
                     e.matmul(SP[:, 0:512], kh[:, kc * 128:(kc + 1) * 128],
                              qt[:, :], start=True, stop=True),
                     waits=w_mm1)
                mm1 = step("tensor",
                           lambda e, SP=SP, kh=kh, kc=2 * kp + 1, qt=qt:
                           e.matmul(SP[:, 512:1024], kh[:, kc * 128:(kc + 1) * 128],
                                    qt[:, :], start=True, stop=True),
                           waits=(), inc=sPE)
                mm1_last = mm1
                if kp >= 2:
                    emit_pv_pair(kp - 2)
                w_exp = [W(sPE, mm1)]
                if pv_on_p.get(rb, 0) > 0:
                    w_exp.append(W(sPE, pv_on_p[rb]))
                exp_on_bank[rb] = step(
                    "scalar",
                    lambda e, PPB=PPB, SP=SP: e.activation(
                        PPB[:, :], SP[:, :], AF.Exp,
                        bias=consts[:, 0:1], scale=consts[:, 1:2]),
                    waits=w_exp, inc=sACT)
                for half in (0, 1):
                    kc = 2 * kp + half
                    if kc >= nkc - 4:
                        i_ = kc - (nkc - 4)
                        moff = 384 - 128 * i_
                        p_rdy[kc] = (sDVE.sem, step(
                            "vector",
                            lambda e, PPB=PPB, half=half, moff=moff: e.tensor_tensor(
                                PPB[:, half * 512:half * 512 + 512],
                                PPB[:, half * 512:half * 512 + 512],
                                cmask_sb[:, moff:moff + 512], ALU.mult),
                            waits=[W(sACT, exp_on_bank[rb]), W(sL, L_cmask)], inc=sDVE))
                    else:
                        p_rdy[kc] = (sACT.sem, exp_on_bank[rb])
            emit_pv_pair(npairs - 2)
            emit_pv_pair(npairs - 1)
            qt_free[it] = mm1_last
            pv_last = sPE.n

            def recf(e):
                e.activation(rec0[:, :], PR[:, :], AF.Ln)
                e.drain()
                return e.activation(rec1[:, :], rec0[:, :], AF.Exp, scale=consts[:, 5:6])
            w_rec = [W(sPE, pv_last)]
            if at_cons > 0:
                w_rec.append(W(sDVE, at_cons))
            rec_free = step("scalar", recf, waits=w_rec, inc=sACT)

            w_at = [W(sACT, rec_free), W(sPE, pv_last)]
            if at_store.get(it - 2):
                w_at.append(W(sG, at_store[it - 2]))
            at_cons = step("vector",
                           lambda e, at=at: e.tensor_tensor(at[:, :], PO[:, :],
                                                            rec1[:, :], ALU.mult),
                           waits=w_at, inc=sDVE)
            po_free = at_cons
            at_store[it] = dma("gpsimd", sG,
                               lambda h=h, qb=qb: aga_in.ap()[h * 128:(h + 1) * 128,
                                                              qb * 512:(qb + 1) * 512],
                               lambda at=at: at[:, :],
                               waits=[W(sDVE, at_cons)])
            it += 1
        kh_free[h] = qt_free[it - 1]
        h_store = sG.n
        sCC.n += 1
        def agh_f(e, h=h, h_store=h_store):
            e.wait_ge(sG.sem, h_store)
            return e.collective_compute("AllGather", ALU.bypass,
                                        replica_groups=[list(range(NCO))],
                                        ins=[aga_in.ap()[h * 128:(h + 1) * 128, :]],
                                        outs=[agh_out[h].ap()])
        P.on("gpsimd", lambda e, h=h, h_store=h_store: agh_f(e, h, h_store).then_inc(sCC.sem, 1))
    c_pe_last = sPE.n
    c_dve_last = sDVE.n
    c_act_last = sACT.n
    at_all = sG.n

    CC_A = sCC.n

    d_entry_w = [W(sPE, c_pe_last), W(sACT, c_act_last), W(sDVE, c_dve_last),
                 W(sG, G_ffnw), W(sG2, G_wall)]
    wallT_ev = 0
    for g_ in range(G):
        ptr = ptr_a if g_ % 2 == 0 else ptr_b
        w_tr = list(d_entry_w) if g_ == 0 else []
        if g_ >= 2:
            w_tr.append(W(sDVE, wallT_ev - 1))
        step("tensor",
             lambda e, g_=g_, ptr=ptr: e.transpose(
                 ptr[:, 0:24], wall_sb[0:24, g_ * 128:(g_ + 1) * 128], eye_sb[0:24, 0:24]),
             waits=w_tr, inc=sPE)
        trc = sPE.n
        w_ev = [W(sPE, trc)] + ([W(sDVE, c_dve_last)] if g_ == 0 else [])
        wallT_ev = step("vector",
                        lambda e, g_=g_, ptr=ptr: e.tensor_copy(
                            wallT[:, g_ * 24:(g_ + 1) * 24], ptr[:, 0:24]),
                        waits=w_ev, inc=sDVE)

    bb_pe = step("tensor",
                 lambda e: e.matmul(PR[:, 0:24], onesf[:, :], basesr[:, :],
                                    start=True, stop=True),
                 waits=[W(sACT, c_act_last), W(sL2, L2_const)], inc=sPE)
    bb_ev = step("vector", lambda e: e.tensor_copy(basesb[:, :], PR[:, 0:24]),
                 waits=[W(sPE, bb_pe)], inc=sDVE)

    gact = step("scalar",
                lambda e: e.activation(gatesb[0:1, 1:2], gatesb[0:1, 0:1], AF.Exp,
                                       scale=consts[0:1, 5:6]),
                waits=[W(sL2, L2_const), W(sPL, P_const)], inc=sACT)

    def gate_f(e):
        e.tensor_scalar(gatesb[0:1, 2:3], gatesb[0:1, 1:2], 1.0, None, ALU.add)
        e.drain()
        e.reciprocal(gatesb[0:1, 3:4], gatesb[0:1, 2:3])
        e.drain()
        e.tensor_copy(gcat[0:1, 0:1], gatesb[0:1, 3:4])
        return e.tensor_tensor(gcat[0:1, 1:2], gatesb[0:1, 1:2], gatesb[0:1, 3:4],
                               ALU.mult)
    g_dve = step("vector", gate_f, waits=[W(sACT, gact)], inc=sDVE)
    gb_pe = step("tensor",
                 lambda e: e.matmul(PR[:, 32:34], onesf[:, :], gcat[:, :],
                                    start=True, stop=True),
                 waits=[W(sDVE, g_dve), W(sDVE, bb_ev)], inc=sPE)
    gb_ev = step("vector", lambda e: e.tensor_copy(gb[:, :], PR[:, 32:34]),
                 waits=[W(sPE, gb_pe)], inc=sDVE)
    def i16_f(e):
        e.memset(i16b[:, :], 0.0)
        e.drain()
        ins = None
        for pos in (0, 5, 10, 15):
            ins = e.tensor_copy(
                i16b[:, :].rearrange("p (t j) -> p t j", j=16)[:, :, pos],
                gb[:, 1:2].broadcast_to((128, 8)))
        return ins
    i16_done = step("vector", i16_f, waits=[W(sDVE, gb_ev)], inc=sDVE)

    cur_stage[0] = 'dlog'
    sv2_free = {}
    lg_ev = 0
    for tt in range(NTT):
        svb = sv2 if tt % 2 == 0 else sv3
        w = [W(sG, G_xcm)]
        if tt < 2:
            w.append(W(sPE, qk_all_pe))
        if sv2_free.get(tt - 2):
            w.append(W(sPE, sv2_free[tt - 2]))
        ld_sv = dma("sync", sL,
                    lambda svb=svb: svb[:, :].rearrange("p (g t) -> p g t", g=G),
                    lambda tt=tt: xn_cm_d.ap()
                        .rearrange("(g p) t -> p g t", p=128)[:, :, tt * 128:(tt + 1) * 128],
                    waits=w)

        def lg_mm(e, svb=svb):
            ins = None
            for g_ in range(G):
                ins = e.matmul(PPAIR[0][:, 0:24], svb[:, g_ * 128:(g_ + 1) * 128],
                               wallT[:, g_ * 24:(g_ + 1) * 24],
                               start=(g_ == 0), stop=(g_ == G - 1))
            return ins
        w_lg = [W(sL, ld_sv), W(sDVE, wallT_ev)]
        if tt == 0:
            w_lg += [W(sACT, c_act_last)]
        if lg_ev > 0:
            w_lg.append(W(sDVE, lg_ev))
        sv2_free[tt] = step("tensor", lg_mm, waits=w_lg, inc=sPE)
        if "dbgf" in _TAPS and tt == 0:
            dma("gpsimd", sG, lambda svb=svb: dbgf.ap()[:, 4096:8192],
                lambda svb2=svb: svb2[:, :], waits=list(w_lg))

        def lg_ev_f(e, tt=tt):
            e.tensor_scalar(logits[:, tt * 24:(tt + 1) * 24], PPAIR[0][:, 0:24],
                            rmsr8[:, tt:tt + 1], None, ALU.mult)
            e.drain()
            return e.tensor_tensor(logits[:, tt * 24:(tt + 1) * 24],
                                   logits[:, tt * 24:(tt + 1) * 24], basesb[:, :], ALU.add)
        lg_ev = step("vector", lg_ev_f,
                     waits=[W(sPE, sv2_free[tt]), W(sACT, rmsr_done[NTT - 1]),
                            W(sDVE, bb_ev)], inc=sDVE)
    cur_stage[0] = 'dsink'
    lg_all = sDVE.n

    lg4 = logits[:, :].rearrange("p (t n) -> p t n", n=24)
    zv4 = zv[:, :].rearrange("p (t i j) -> p t i j", i=4, j=4)
    s14 = s1b[:, :].rearrange("p (t i j) -> p t i j", i=4, j=4)
    s24 = s2b[:, :].rearrange("p (t i j) -> p t i j", i=4, j=4)
    e14 = e1b[:, :].rearrange("p (t i j) -> p t i j", i=4, j=4)
    su3 = sub_[:, :].rearrange("p (t i) -> p t i", i=4)
    sv3 = svb_[:, :].rearrange("p (t j) -> p t j", j=4)
    ul3 = ulb[:, :].rearrange("p (t i) -> p t i", i=4)
    vl3 = vlb[:, :].rearrange("p (t j) -> p t j", j=4)

    def sink_init(e):
        e.tensor_scalar(zv[:, :].rearrange("p (t i j) -> p t i j", i=4, j=4),
                        lg4[:, :, 0:16].rearrange("p t (i j) -> p t i j", j=4),
                        10.0, None, ALU.mult)
        return e.memset(vlb[:, :], 0.0)
    sk = step("vector", sink_init, waits=[W(sDVE, lg_all)], inc=sDVE)

    for itr in range(SINK_ITERS):
        sk = step("vector",
                  lambda e: e.tensor_tensor(s14, zv4, vl3.unsqueeze(2).broadcast_to(
                      (128, 8, 4, 4)), ALU.subtract),
                  waits=[W(sDVE, sk)] if itr == 0 else [W(sACT, sk)], inc=sDVE)
        sk = step("scalar", lambda e: e.activation(e1b[:, :], s1b[:, :], AF.Exp),
                  waits=[W(sDVE, sk)], inc=sACT)
        sk = step("vector", lambda e: e.tensor_reduce(su3, e14, AX.X, ALU.add),
                  waits=[W(sACT, sk)], inc=sDVE)
        sk = step("scalar", lambda e: e.activation(ulb[:, :], sub_[:, :], AF.Ln),
                  waits=[W(sDVE, sk)], inc=sACT)
        sk = step("vector",
                  lambda e: e.tensor_tensor(s24, zv4, ul3.unsqueeze(3).broadcast_to(
                      (128, 8, 4, 4)), ALU.subtract),
                  waits=[W(sACT, sk)], inc=sDVE)
        sk = step("scalar", lambda e: e.activation(e1b[:, :], s2b[:, :], AF.Exp),
                  waits=[W(sDVE, sk)], inc=sACT)

        def vsum_f(e):
            e.tensor_tensor(sv3, e14[:, :, 0, :], e14[:, :, 1, :], ALU.add)
            e.drain()
            e.tensor_tensor(sv3, sv3, e14[:, :, 2, :], ALU.add)
            e.drain()
            return e.tensor_tensor(sv3, sv3, e14[:, :, 3, :], ALU.add)
        sk = step("vector", vsum_f, waits=[W(sACT, sk)], inc=sDVE)
        sk = step("scalar", lambda e: e.activation(vlb[:, :], svb_[:, :], AF.Ln),
                  waits=[W(sDVE, sk)], inc=sACT)

    sk = step("vector",
              lambda e: e.tensor_tensor(s14, s24, vl3.unsqueeze(2).broadcast_to(
                  (128, 8, 4, 4)), ALU.subtract),
              waits=[W(sACT, sk)], inc=sDVE)
    sk = step("scalar", lambda e: e.activation(hres[:, :], s1b[:, :], AF.Exp),
              waits=[W(sDVE, sk)], inc=sACT)
    hres_done = step("vector",
                     lambda e: e.scalar_tensor_tensor(hres[:, :], hres[:, :],
                                                      gb[:, 0:1], i16b[:, :],
                                                      ALU.mult, ALU.add),
                     waits=[W(sACT, sk), W(sDVE, i16_done)], inc=sDVE)

    hh = {}
    for name, off, buf in (("pre", 16, hpre), ("post", 20, hpost)):
        w_ea = [W(sDVE, lg_all)]
        if "dbgf" in _TAPS:
            w_ea.append(W(sG, sG.n))
        if name == "post":
            w_ea.append(W(sDVE, hh["pre"]))
        ea = step("scalar",
                  lambda e, off=off: e.activation(
                      hwork[:, :].rearrange("p (t j) -> p t j", j=4),
                      lg4[:, :, off:off + 4], AF.Exp),
                  waits=w_ea, inc=sACT)

        def sm_f(e, buf=buf):
            e.tensor_reduce(rsum8[:, :], hwork[:, :].rearrange("p (t j) -> p t j", j=4),
                            AX.X, ALU.add)
            e.drain()
            e.reciprocal(rcp8[:, :], rsum8[:, :])
            e.drain()
            return e.tensor_tensor(buf[:, :].rearrange("p (t j) -> p t j", j=4),
                                   hwork[:, :].rearrange("p (t j) -> p t j", j=4),
                                   rcp8[:, :].rearrange("p t -> p t").unsqueeze(2)
                                   .broadcast_to((128, 8, 4)), ALU.mult)
        hh[name] = step("vector", sm_f, waits=[W(sACT, ea)], inc=sDVE)
    h_all = sDVE.n
    if "dbgf" in _TAPS:
        dma("gpsimd", sG, lambda: dbgf.ap()[:, 0:192], lambda: logits[:, :],
            waits=[W(sDVE, h_all)])
        dma("gpsimd", sG, lambda: dbgf.ap()[:, 192:320], lambda: hres[:, :],
            waits=[W(sDVE, h_all)])
        dma("gpsimd", sG, lambda: dbgf.ap()[:, 320:352], lambda: hpre[:, :],
            waits=[W(sDVE, h_all)])
        dma("gpsimd", sG, lambda: dbgf.ap()[:, 352:384], lambda: hpost[:, :],
            waits=[W(sDVE, h_all)])
        dma("gpsimd", sG, lambda: dbgf.ap()[:, 384:392], lambda: rsum8[:, :],
            waits=[W(sDVE, lg_all)])
        dma("gpsimd", sG, lambda: dbgf.ap()[:, 392:400], lambda: rcp8[:, :],
            waits=[W(sDVE, lg_all)])
        dma("gpsimd", sG, lambda: dbgf.ap()[:, 504:512], lambda: rmsr8[:, :],
            waits=[W(sDVE, lg_all)])

    cur_stage[0] = 'dapp'
    xnm_free2 = {}
    ffn_st = {}
    y_free = {0: 0, 1: 0}
    ffnt_free = 0
    xpre_tr_ev = {0: 0, 1: 0}
    for tt in range(NTT):
        xnm = xnm0 if tt % 2 == 0 else xnm1
        w_x = [W(sDVE, lg_all), W(sPE, sv2_free[NTT - 1])]
        if xnm_free2.get(tt - 2):
            w_x.append(W(sDVE, xnm_free2[tt - 2]))
        ld_xn = dma("sync", sL, lambda xnm=xnm: xnm[:, :],
                    lambda tt=tt: xn_tm_d.ap()[tt * 128:(tt + 1) * 128, :],
                    waits=w_x)

        def xpre_fn(e, xnm=xnm, tt=tt):
            e.tensor_scalar(xpre_f[:, :], xnm[:, 0:1024], hpre[:, tt * 4:tt * 4 + 1],
                            None, ALU.mult)
            for n in (1, 2, 3):
                e.drain()
                e.scalar_tensor_tensor(xpre_f[:, :], xnm[:, n * 1024:(n + 1) * 1024],
                                       hpre[:, tt * 4 + n:tt * 4 + n + 1], xpre_f[:, :],
                                       ALU.mult, ALU.add)
            e.drain()
            return e.tensor_copy(xpre_b[:, :], xpre_f[:, :])
        w_xp = [W(sL, ld_xn), W(sDVE, h_all), W(sDVE, hres_done)]
        xp = step("vector", xpre_fn, waits=w_xp, inc=sDVE)

        for j in range(8):
            ptr = ptr_a if j % 2 == 0 else ptr_b
            w_tr = [W(sDVE, xp)]
            if xpre_tr_ev[j % 2] > 0:
                w_tr.append(W(sDVE, xpre_tr_ev[j % 2]))
            step("tensor",
                 lambda e, j=j, ptr=ptr: e.transpose(
                     ptr, xpre_b[:, j * 128:(j + 1) * 128], eye_sb[:, :]),
                 waits=w_tr, inc=sPE)
            trc = sPE.n
            xpre_tr_ev[j % 2] = step(
                "vector",
                lambda e, j=j, ptr=ptr: e.tensor_copy(
                    xpre_cm[:, j * 128:(j + 1) * 128], ptr),
                waits=[W(sPE, trc)], inc=sDVE)
        xpc = sDVE.n

        y_pe = {}
        for cb in range(2):
            w_y = [W(sDVE, xpc), W(sG, G_wbr)]
            if y_free[cb] > 0:
                w_y.append(W(sDVE, y_free[cb]))

            def ymm(e, cb=cb):
                ins = None
                for ct in range(8):
                    ins = e.matmul(PPAIR[2][:, cb * 512:cb * 512 + 512], xpre_cm[:, ct * 128:(ct + 1) * 128],
                                   wbr_sb[:, ct * 1024 + cb * 512: ct * 1024 + cb * 512 + 512],
                                   start=(ct == 0), stop=(ct == 7))
                return ins
            y_pe[cb] = step("tensor", ymm, waits=w_y, inc=sPE)

        last_comb = 0
        for n in range(4):
            for cb in range(2):
                def comb_f(e, n=n, cb=cb, tt=tt, xnm=xnm):
                    sl = slice(cb * 512, cb * 512 + 512)
                    e.tensor_scalar(xpre_f[:, sl], PPAIR[2][:, cb * 512:cb * 512 + 512],
                                    hpost[:, tt * 4 + n:tt * 4 + n + 1], None, ALU.mult)
                    ins = None
                    for j in range(4):
                        e.drain()
                        dst = ffnt[:, n * 1024 + cb * 512: n * 1024 + cb * 512 + 512] \
                            if j == 3 else xpre_f[:, sl]
                        ins = e.scalar_tensor_tensor(
                            dst, xnm[:, j * 1024 + cb * 512: j * 1024 + cb * 512 + 512],
                            hres[:, tt * 16 + n * 4 + j: tt * 16 + n * 4 + j + 1],
                            xpre_f[:, sl], ALU.mult, ALU.add)
                    return ins
                w_c = [W(sPE, y_pe[cb])]
                if n == 0 and cb == 0 and ffnt_free > 0:
                    w_c.append(W(sG, ffnt_free))
                last_comb = step("vector", comb_f, waits=w_c, inc=sDVE)
        y_free[0] = last_comb
        y_free[1] = last_comb
        xnm_free2[tt] = last_comb
        ffn_st[tt] = dma("gpsimd", sG,
                         lambda tt=tt: ffn_d.ap()[tt * 128:(tt + 1) * 128, :],
                         lambda: ffnt[:, :],
                         waits=[W(sDVE, last_comb)])
        ffnt_free = ffn_st[tt]
    d_pe_last = sPE.n
    d_dve_last = sDVE.n
    ffn_all_st = sG.n

    cur_stage[0] = 'e'
    def asb_ld_f(e):
        e.wait_ge(sCC.sem, CC_A)
        e.wait_ge(sPE.sem, c_pe_last)
        pid = nc.partition_id(engines=[mybir.EngineType.SP])
        off = e.compute_val(pid * TPC)
        for hh in range(NH):
            e.dma_start(
                out=a_sb[:, :].rearrange("p (r hh t) -> p r hh t", hh=NH, t=TPC)[:, :, hh, :],
                in_=agh_out[hh].ap().rearrange("(r p) t -> p r t", p=128)[:, :, bass.ds(off, TPC)]
            ).then_inc(sL.sem, 16)
    sL.n += 64
    P.on("sync", asb_ld_f)
    L_asb = sL.n

    wo_ld = {}
    wo_free = {}
    for mb in range(2):
        wob = wo_b0 if mb % 2 == 0 else wo_b1
        wo_ld[mb] = dma("gpsimd", sG,
                        lambda wob=wob: wob[:, :].rearrange("p (g n) -> p g n", g=G),
                        lambda mb=mb: wo_in.ap()[:, mb * 512:(mb + 1) * 512]
                            .rearrange("(g p) n -> p g n", p=128),
                        waits=[W(sPE, d_pe_last), W(sDVE, d_dve_last)])

    os_free = {0: 0, 1: 0}
    st_out = {}
    e_mm_done = {}
    for mb in range(8):
        wob = wo_b0 if mb % 2 == 0 else wo_b1
        if mb >= 2:
            wo_ld[mb] = dma("gpsimd", sG,
                            lambda wob=wob: wob[:, :].rearrange("p (g n) -> p g n", g=G),
                            lambda mb=mb: wo_in.ap()[:, mb * 512:(mb + 1) * 512]
                                .rearrange("(g p) n -> p g n", p=128),
                            waits=[W(sPE, e_mm_done[(mb - 2, NTT - 1)])])
        for tt in range(NTT):
            k = mb * NTT + tt
            bank = 6 + k % 2
            xsb = xs0 if k % 2 == 0 else xs1
            fsb = fs0 if k % 2 == 0 else fs1
            osb = os0 if k % 2 == 0 else os1
            w_xs = [W(sPE, c_pe_last)]
            if k >= 2:
                w_xs.append(W(sDVE, os_free[k % 2]))
            ld_xs = dma("sync", sL, lambda xsb=xsb: xsb[:, :],
                        lambda tt=tt, mb=mb: x_in.ap()[tt * 128:(tt + 1) * 128,
                                                       mb * 512:(mb + 1) * 512],
                        waits=w_xs)
            ld_fs = dma("sync", sL, lambda fsb=fsb: fsb[:, :],
                        lambda tt=tt, mb=mb: ffn_d.ap()[tt * 128:(tt + 1) * 128,
                                                        mb * 512:(mb + 1) * 512],
                        waits=[W(sG, ffn_st[tt])])

            def emm(e, tt=tt, bank=bank, wob=wob):
                ins = None
                for g_ in range(G):
                    ins = e.matmul(PB[bank],
                                   a_sb[:, g_ * TPC + tt * 128: g_ * TPC + (tt + 1) * 128],
                                   wob[:, g_ * 512:(g_ + 1) * 512],
                                   start=(g_ == 0), stop=(g_ == G - 1))
                return ins
            w_mm = [W(sL, L_asb), W(sG, wo_ld[mb])]
            if k == 0:
                w_mm += [W(sPE, d_pe_last), W(sDVE, d_dve_last)]
            if k >= 2:
                w_mm.append(W(sDVE, os_free[k % 2]))
            e_mm_done[(mb, tt)] = step("tensor", emm, waits=w_mm, inc=sPE)

            def add_f(e, osb=osb, xsb=xsb, fsb=fsb, bank=bank):
                e.tensor_tensor(osb[:, :], PB[bank], xsb[:, :], ALU.add)
                e.drain()
                return e.tensor_tensor(osb[:, :], osb[:, :], fsb[:, :], ALU.add)
            w_add = [W(sPE, e_mm_done[(mb, tt)]), W(sL, ld_xs), W(sL, ld_fs)]
            if k >= 2 and (k - 2) in st_out:
                w_add.append(W(sL, st_out[k - 2]))
            os_free[k % 2] = step("vector", add_f, waits=w_add, inc=sDVE)
            st_out[k] = dma("sync", sL,
                            lambda tt=tt, mb=mb: out_t.ap()[tt * 128:(tt + 1) * 128,
                                                            mb * 512:(mb + 1) * 512],
                            lambda osb=osb: osb[:, :],
                            waits=[W(sDVE, os_free[k % 2])])

    for (tout, srct) in tap_outs:
        dma("gpsimd", sG2, lambda tout=tout: tout.ap(), lambda srct=srct: srct.ap(),
            waits=[W(sL, sL.n), W(sG, sG.n), W(sDVE, sDVE.n), W(sACT, sACT.n)])

    fin_L, fin_G, fin_G2 = sL.n, sG.n, sG2.n
    P.on("sync", lambda e: e.wait_ge(sL.sem, fin_L))
    P.on("gpsimd", lambda e: e.wait_ge(sG.sem, fin_G))
    P.on("gpsimd", lambda e: e.wait_ge(sG2.sem, fin_G2))

    with nc.Block() as blk:
        P.flush(blk)
    es.close()
    return nc


_cache = {}


def kernel(**inputs):
    bf = ml_dtypes.bfloat16
    x = np.asarray(inputs["L0_attn_hidden_states"])
    wq = np.asarray(inputs["L0_attn_wq"])
    wk = np.asarray(inputs["L0_attn_wk"])
    wv = np.asarray(inputs["L0_attn_wv"])
    wo = np.ascontiguousarray(np.asarray(inputs["L0_attn_wo"]))
    wbr = np.ascontiguousarray(np.asarray(inputs["L0_ffn_W_branch"]))
    wall = np.ascontiguousarray(np.concatenate([
        np.asarray(inputs["L0_ffn_W_res"]),
        np.asarray(inputs["L0_ffn_W_pre"]),
        np.asarray(inputs["L0_ffn_W_post"])], axis=0))
    bases = np.ascontiguousarray(np.concatenate([
        np.asarray(inputs["L0_ffn_H_res_base"]).reshape(-1),
        np.asarray(inputs["L0_ffn_H_pre_base"]).reshape(-1),
        np.asarray(inputs["L0_ffn_H_post_base"]).reshape(-1)])[None, :]).astype(np.float32)
    gate = np.ascontiguousarray(np.asarray(inputs["L0_ffn_gate"]).reshape(1, 1)).astype(np.float32)

    eye = np.eye(128, dtype=np.float32).astype(bf)
    ones = np.ones((128, 128), dtype=np.float32).astype(bf)
    iota_d = (np.arange(128, dtype=np.int64) % 64).astype(np.float32)[None, :]
    iota_t = np.arange(S, dtype=np.int64).astype(np.float32)[None, :]
    k_idx = np.arange(128)[:, None]
    j_idx = np.arange(1024)[None, :]
    cmask = np.ascontiguousarray(((j_idx - 384) >= k_idx).astype(np.float32).astype(bf))

    if "nc" not in _cache:
        _cache["nc"] = build_program()
    nc = _cache["nc"]

    in_maps = []
    for c in range(NCO):
        in_maps.append({
            "x_slice": np.ascontiguousarray(x[0, c * TPC:(c + 1) * TPC, :]),
            "wq_s": np.ascontiguousarray(wq[:, c * 512:(c + 1) * 512]),
            "wk_s": np.ascontiguousarray(wk[:, c * 512:(c + 1) * 512]),
            "wv_s": np.ascontiguousarray(wv[:, c * 512:(c + 1) * 512]),
            "wo": wo, "wbr": wbr, "wall": wall, "bases": bases, "gate": gate,
            "eye": eye, "onesb": ones, "iota_d": iota_d, "iota_t": iota_t,
            "cmask": cmask,
        })
    res = run_bass_kernel_spmd(nc, in_maps, core_ids=list(range(NCO)))
    if _TAPS:
        _cache["taps"] = res
    out = np.concatenate([res.results[c]["out"] for c in range(NCO)], axis=0)
    return out.reshape(1, S, D)


# version 10
# speedup vs baseline: 1.0975x; 1.0099x over previous
import numpy as np
import ml_dtypes
from contextlib import ExitStack

import concourse.bass as bass
import concourse.mybir as mybir
from concourse.bass_utils import run_bass_kernel_spmd

F32 = mybir.dt.float32
BF16 = mybir.dt.bfloat16
I32 = mybir.dt.int32
AF = mybir.ActivationFunctionType
ALU = mybir.AluOpType
AX = mybir.AxisListType

NCO = 8
S = 8192
D = 4096
TPC = S // NCO
NTT = TPC // 128
G = D // 128
NH = 4
TC = S // 512
VT = S // 128
QB = 16
SCALE = float(1.0 / np.sqrt(128.0))
EXP_BIAS = -10.0
LN_EPS = 1e-5
RMS_EPS = 1.1920929e-07
SINK_ITERS = 50

_TAPS = ()
_SERIAL = False
_SERIAL_STAGE = None
_SERIAL_MASK = 'PADLG'

ENGS = ("sync", "gpsimd", "tensor", "vector", "scalar")


class Ctr:
    def __init__(self, es, nc, name):
        self.sem = es.enter_context(nc.semaphore(name))
        self.n = 0


class Prog:

    def __init__(self):
        self.q = {e: [] for e in ENGS}

    def on(self, eng, fn):
        self.q[eng].append(fn)

    def flush(self, blk):
        for eng in ENGS:
            fns = self.q[eng]
            getattr(blk, eng)(lambda e, fns=fns: [f(e) for f in fns])


def build_program(detect_races=True):
    nc = bass.Bass("TRN2", target_bir_lowering=False, detect_race_conditions=detect_races)
    es = ExitStack()
    P = Prog()

    x_in = nc.dram_tensor("x_slice", [TPC, D], F32, kind="ExternalInput")
    wq_in = nc.dram_tensor("wq_s", [D, 512], F32, kind="ExternalInput")
    wk_in = nc.dram_tensor("wk_s", [D, 512], F32, kind="ExternalInput")
    wv_in = nc.dram_tensor("wv_s", [D, 512], F32, kind="ExternalInput")
    wo_in = nc.dram_tensor("wo", [D, D], F32, kind="ExternalInput")
    wbr_in = nc.dram_tensor("wbr", [1024, 1024], F32, kind="ExternalInput")
    wall_in = nc.dram_tensor("wall", [24, D], F32, kind="ExternalInput")
    bases_in = nc.dram_tensor("bases", [1, 24], F32, kind="ExternalInput")
    gate_in = nc.dram_tensor("gate", [1, 1], F32, kind="ExternalInput")
    eye_in = nc.dram_tensor("eye", [128, 128], BF16, kind="ExternalInput")
    ones_in = nc.dram_tensor("onesb", [128, 128], BF16, kind="ExternalInput")
    iota_d_in = nc.dram_tensor("iota_d", [1, 128], F32, kind="ExternalInput")
    iota_t_in = nc.dram_tensor("iota_t", [1, S], F32, kind="ExternalInput")
    cmask_in = nc.dram_tensor("cmask", [128, 1024], BF16, kind="ExternalInput")

    out_t = nc.dram_tensor("out", [TPC, D], F32, kind="ExternalOutput")

    agx_in = [nc.dram_tensor("agx_in%d" % i, [D, 512], BF16) for i in range(2)]
    agx_out = [nc.dram_tensor("agx_out%d" % i, [NCO * D, 512], BF16) for i in range(2)]
    xn_cm_d = nc.dram_tensor("xn_cm_d", [D, TPC], BF16)
    xn_tm_d = nc.dram_tensor("xn_tm_d", [TPC, D], BF16)
    q_dram = nc.dram_tensor("q_dram", [512, S], BF16)
    k_dram = nc.dram_tensor("k_dram", [512, S], BF16)
    aga_in = nc.dram_tensor("aga_in", [512, S], BF16)
    agh_out = [nc.dram_tensor("agh_out%d" % h, [NCO * 128, S], BF16) for h in range(NH)]
    ffn_d = nc.dram_tensor("ffn_d", [TPC, D], BF16)
    dbgf = nc.dram_tensor("dbgf", [128, 8192], F32)
    _tapmap = {"agx_out": agx_out[0], "xn_tm_d": xn_tm_d, "q_dram": q_dram,
               "k_dram": k_dram, "aga_in": aga_in, "aga_out": agh_out[0],
               "ffn_d": ffn_d, "xn_cm_d": xn_cm_d, "dbgf": dbgf}
    tap_outs = []
    for tn in [t for t in _TAPS if t != "dbgf"]:
        src = _tapmap[tn]
        tap_outs.append((nc.dram_tensor("tap_" + tn, list(src.shape), BF16,
                                        kind="ExternalOutput") if src is not dbgf else
                         nc.dram_tensor("tap_" + tn, list(src.shape), F32,
                                        kind="ExternalOutput"), src))

    SB_BASE = 16512

    def A(name, off, shape, dt):
        return nc.alloc_sbuf_tensor_at(name, shape, dt, offset=SB_BASE + off)

    xn_cm = A("xn_cm", 0, [128, G * TPC], BF16)
    v_sb = A("v_sb", 0, [128, VT * 512], BF16)
    a_sb = A("a_sb", 0, [128, G * TPC], BF16)

    wq_sb = A("wq_sb", 65536, [128, G * 512], BF16)
    wk_sb = A("wk_sb", 98304, [128, G * 512], BF16)
    wo_b0 = A("wo_b0", 65536, [128, G * 512], BF16)
    wo_b1 = A("wo_b1", 98304, [128, G * 512], BF16)
    wbr_sb = A("wbr_sb", 65536, [128, 8 * 1024], BF16)
    xnm0 = A("xnm0", 81920, [128, D], BF16)
    xnm1 = A("xnm1", 90112, [128, D], BF16)
    wall_sb = A("wall_sb", 98304, [24, D], BF16)
    xpre_f = A("xpre_f", 106496, [128, 1024], F32)
    xpre_b = A("xpre_b", 110592, [128, 1024], BF16)
    xpre_cm = A("xpre_cm", 112640, [128, 1024], BF16)
    dso = 114688
    logits = A("logits", dso, [128, 8 * 24], F32); dso += 768
    zv = A("zv", dso, [128, 128], F32); dso += 512
    s1b = A("s1b", dso, [128, 128], F32); dso += 512
    s2b = A("s2b", dso, [128, 128], F32); dso += 512
    e1b = A("e1b", dso, [128, 128], F32); dso += 512
    sub_ = A("sub_", dso, [128, 32], F32); dso += 128
    svb_ = A("svb_", dso, [128, 32], F32); dso += 128
    ulb = A("ulb", dso, [128, 32], F32); dso += 128
    vlb = A("vlb", dso, [128, 32], F32); dso += 128
    hres = A("hres", dso, [128, 128], F32); dso += 512
    i16b = A("i16b", dso, [128, 128], F32); dso += 512
    hpre = A("hpre", dso, [128, 32], F32); dso += 128
    hpost = A("hpost", dso, [128, 32], F32); dso += 128
    hwork = A("hwork", dso, [128, 32], F32); dso += 128
    rsum8 = A("rsum8", dso, [128, 8], F32); dso += 32
    rcp8 = A("rcp8", dso, [128, 8], F32); dso += 32
    wallT = A("wallT", dso, [128, G * 24], BF16); dso += 1536
    basesb = A("basesb", dso, [128, 24], F32); dso += 96
    gb = A("gb", dso, [128, 2], F32); dso += 32
    assert dso <= 131072, dso

    cos_sb = A("cos_sb", 131072, [128, S], BF16)
    sin_sb = A("sin_sb", 147456, [128, S], BF16)
    sv0 = A("sv0", 131072, [128, G * 128], BF16)
    sv1 = A("sv1", 139264, [128, G * 128], BF16)
    kh0 = A("kh0", 131072, [128, S], BF16)
    kh1 = A("kh1", 147456, [128, S], BF16)
    xs0 = A("xs0", 131072, [128, 512], F32)
    xs1 = A("xs1", 133120, [128, 512], F32)
    fs0 = A("fs0", 135168, [128, 512], BF16)
    fs1 = A("fs1", 136192, [128, 512], BF16)
    os0 = A("os0", 137216, [128, 512], F32)
    os1 = A("os1", 139264, [128, 512], F32)
    sv3 = A("sv3", 81920, [128, G * 128], BF16)

    xb0 = A("xb0", 163840, [128, 2048], F32)
    xb1 = A("xb1", 172032, [128, 2048], F32)
    xnt0 = A("xnt0", 180224, [128, D], BF16)
    xnt1 = A("xnt1", 188416, [128, D], BF16)
    mv0 = A("mv0", 163840, [128, 16 * 512], BF16)
    mv1 = A("mv1", 180224, [128, 16 * 512], BF16)
    wv_sb = A("wv_sb", 163840, [128, G * 512], BF16)
    co = 163840
    pb3 = [A("pb3_%d" % i, co + 2048 * i, [128, 1024], BF16) for i in range(3)]; co += 6144
    qt0 = A("qt0", co, [128, 512], BF16); co += 1024
    qt1 = A("qt1", co, [128, 512], BF16); co += 1024
    at0 = A("at0", co, [128, 512], BF16); co += 1024
    at1 = A("at1", co, [128, 512], BF16); co += 1024
    rec0 = A("rec0", co, [128, 512], F32); co += 2048
    rec1 = A("rec1", co, [128, 512], F32); co += 2048
    cmask_sb = A("cmask_sb", co, [128, 1024], BF16); co += 2048
    sv2 = A("sv2", co, [128, G * 128], BF16); co += 8192
    ffnt = A("ffnt", co, [128, D], BF16); co += 8192
    assert co <= 196608, co

    so = 196608
    ones_sb = A("ones_sb", so, [128, 128], BF16); so += 256
    eye_sb = A("eye_sb", so, [128, 128], BF16); so += 256
    consts = A("consts", so, [128, 8], F32); so += 32
    stats = A("stats", so, [128, 8], F32); so += 32
    onesf = A("onesf", so, [1, 128], F32); so += 512
    invf = A("invf", so, [1, 128], F32); so += 512
    iotc = A("iotc", so, [1, 512], F32); so += 2048
    frq0 = A("frq0", so, [128, 512], F32); so += 2048
    frq1 = A("frq1", so, [128, 512], F32); so += 2048
    kint = A("kint", so, [128, 512], I32); so += 2048
    prt = A("prt", so, [128, 512], F32); so += 2048
    rt1 = A("rt1", so, [128, 512], F32); so += 2048
    qkb0 = A("qkb0", so, [128, 512], BF16); so += 1024
    qkb1 = A("qkb1", so, [128, 512], BF16); so += 1024
    gatesb = A("gatesb", so, [1, 4], F32); so += 32
    gcat = A("gcat", so, [1, 2], F32); so += 32
    basesr = A("basesr", so, [1, 24], F32); so += 96
    sxn = A("sxn", so, [128, 2], F32); so += 32
    rmsr8 = A("rmsr8", so, [128, 8], F32); so += 32
    assert so <= 212800, so

    PPAIR = [es.enter_context(nc.psum_tensor("pp%d" % i, [128, 1024], F32)) for i in range(3)]
    PO = es.enter_context(nc.psum_tensor("po", [128, 512], F32))
    PR = es.enter_context(nc.psum_tensor("pr", [128, 512], F32))
    PB = [PPAIR[0][:, 0:512], PPAIR[0][:, 512:1024],
          PPAIR[1][:, 0:512], PPAIR[1][:, 512:1024],
          PPAIR[2][:, 0:512], PPAIR[2][:, 512:1024],
          PO[:, :], PR[:, :]]
    ptr_a = PPAIR[1][:, :].bitcast(BF16)[:, 0:128]
    ptr_b = PPAIR[1][:, :].bitcast(BF16)[:, 1024:1152]
    ptr_c = PPAIR[2][:, :].bitcast(BF16)[:, 1024:1152]
    ptr_c2 = PPAIR[2][:, :].bitcast(BF16)[:, 0:128]

    sL = Ctr(es, nc, "sL")
    sG = Ctr(es, nc, "sG")
    sCC = Ctr(es, nc, "sCC")
    sPE = Ctr(es, nc, "sPE")
    sACT = Ctr(es, nc, "sACT")
    sDVE = Ctr(es, nc, "sDVE")
    sPL = Ctr(es, nc, "sPL")
    sL2 = Ctr(es, nc, "sL2")
    sG2 = Ctr(es, nc, "sG2")

    def dma(queue, ctr, outf, inf, waits=()):
        waits = list(waits)
        if _SERIAL or cur_stage[0] == 'dlog' or (_SERIAL_STAGE is not None and cur_stage[0] == _SERIAL_STAGE):
            mm = _SERIAL_MASK
            if 'P' in mm: waits.append((sPE.sem, sPE.n))
            if 'A' in mm: waits.append((sACT.sem, sACT.n))
            if 'D' in mm: waits.append((sDVE.sem, sDVE.n))
        ctr.n += 16
        def f(e, outf=outf, inf=inf, waits=tuple(waits), sem=ctr.sem):
            for (s, v) in waits:
                if v > 0:
                    e.wait_ge(s, v)
            e.dma_start(out=outf(), in_=inf()).then_inc(sem, 16)
        P.on(queue, f)
        return ctr.n

    cur_stage = [None]

    def step(eng, fn, waits=(), inc=None):
        waits = list(waits)
        if _SERIAL or cur_stage[0] == 'dlog' or (_SERIAL_STAGE is not None and cur_stage[0] == _SERIAL_STAGE):
            mm = _SERIAL_MASK
            if 'P' in mm: waits.append((sPE.sem, sPE.n))
            if 'A' in mm: waits.append((sACT.sem, sACT.n))
            if 'D' in mm: waits.append((sDVE.sem, sDVE.n))
            if 'L' in mm: waits.append((sL.sem, sL.n))
            if 'G' in mm: waits.append((sG.sem, sG.n))
        if inc is not None:
            inc.n += 1
        def f(e, fn=fn, waits=tuple(waits), inc=inc):
            for (s, v) in waits:
                if v > 0:
                    e.wait_ge(s, v)
            ins = fn(e)
            if inc is not None:
                ins.then_inc(inc.sem, 1)
        P.on(eng, f)
        return inc.n if inc is not None else 0

    W = lambda ctr, v: (ctr.sem, v)

    dma("sync", sL, lambda: eye_sb[:, :], lambda: eye_in.ap())
    L_const = dma("sync", sL, lambda: ones_sb[:, :], lambda: ones_in.ap())
    dma("sync", sL2, lambda: basesr[:, :], lambda: bases_in.ap())
    dma("sync", sL2, lambda: gatesb[0:1, 0:1], lambda: gate_in.ap())
    L2_const = dma("sync", sL2, lambda: invf[:, :], lambda: iota_d_in.ap())

    def memsets(e):
        e.memset(consts[:, 0:1], EXP_BIAS)
        e.memset(consts[:, 1:2], SCALE)
        e.memset(consts[:, 2:3], LN_EPS)
        e.memset(consts[:, 3:4], -0.5)
        e.memset(consts[:, 4:5], RMS_EPS)
        e.memset(consts[:, 5:6], -1.0)
        e.memset(consts[:, 6:7], -float(np.pi))
        return e.memset(onesf[:, :], 1.0)
    P_const = step("gpsimd", memsets, inc=sPL)

    A_invf = step("scalar",
                  lambda e: e.activation(invf[:, :], invf[:, :], AF.Exp,
                                         scale=-float(np.log(10000.0) / 64.0)),
                  waits=[W(sL2, L2_const), W(sPL, P_const)], inc=sACT)

    G_wq = dma("gpsimd", sG,
               lambda: wq_sb[:, :].rearrange("p (g n) -> p g n", g=G),
               lambda: wq_in.ap().rearrange("(g p) n -> p g n", p=128))
    G_wqk = dma("gpsimd", sG,
                lambda: wk_sb[:, :].rearrange("p (g n) -> p g n", g=G),
                lambda: wk_in.ap().rearrange("(g p) n -> p g n", p=128))

    ln_xn_done = [0] * NTT
    rmsr_done = [0] * NTT
    ln_ev_done = [0] * NTT
    ln_st_done = [0] * NTT
    bank_ev = {}
    x_ld = {}

    for tt in range(NTT):
        for half in range(2):
            xb = xb0 if half == 0 else xb1
            w = [W(sDVE, ln_xn_done[tt - 1])] if tt > 0 else []
            x_ld[(tt, half)] = dma(
                "sync", sL,
                lambda xb=xb: xb[:, :],
                lambda tt=tt, half=half: x_in.ap()[tt * 128:(tt + 1) * 128,
                                                  half * 2048:(half + 1) * 2048],
                waits=w)

        xnt = xnt0 if tt % 2 == 0 else xnt1
        def stats_f(e, tt=tt, xnt=xnt):
            for half, xb in ((0, xb0), (1, xb1)):
                e.scalar_tensor_tensor(xnt[:, half * 2048:(half + 1) * 2048],
                                       xb[:, :], 1.0, xb[:, :], ALU.bypass, ALU.mult,
                                       accum_out=stats[:, half:half + 1])
                e.tensor_reduce(stats[:, 2 + half:3 + half], xb[:, :], AX.X, ALU.add)
            e.drain()
            e.tensor_tensor(stats[:, 2:3], stats[:, 2:3], stats[:, 3:4], ALU.add)
            e.tensor_tensor(stats[:, 0:1], stats[:, 0:1], stats[:, 1:2], ALU.add)
            e.drain()
            e.tensor_scalar(stats[:, 2:3], stats[:, 2:3], 1.0 / D, None, ALU.mult)
            e.drain()
            e.tensor_tensor(stats[:, 3:4], stats[:, 2:3], stats[:, 2:3], ALU.mult)
            e.drain()
            return e.tensor_scalar(stats[:, 1:2], stats[:, 0:1], 1.0 / D,
                                   stats[:, 3:4], ALU.mult, ALU.subtract)
        w_st = [W(sL, x_ld[(tt, 0)]), W(sL, x_ld[(tt, 1)]), W(sPL, P_const)]
        if tt > 0:
            w_st.append(W(sACT, rmsr_done[tt - 1]))
        if tt > 1:
            w_st += [W(sDVE, ln_ev_done[tt - 2]), W(sG, ln_st_done[tt - 2])]
        var_done = step("vector", stats_f, waits=w_st, inc=sDVE)

        def rstd_f(e):
            e.activation(stats[:, 5:6], stats[:, 1:2], AF.Ln, bias=consts[:, 2:3])
            e.drain()
            return e.activation(stats[:, 4:5], stats[:, 5:6], AF.Exp, scale=consts[:, 3:4])
        rstd_done = step("scalar", rstd_f, waits=[W(sDVE, var_done)], inc=sACT)

        def xn_f(e, xnt=xnt):
            e.tensor_scalar(xnt[:, 0:2048], xb0[:, :], stats[:, 2:3], stats[:, 4:5],
                            ALU.subtract, ALU.mult)
            return e.tensor_scalar(xnt[:, 2048:4096], xb1[:, :], stats[:, 2:3],
                                   stats[:, 4:5], ALU.subtract, ALU.mult)
        ln_xn_done[tt] = step("vector", xn_f, waits=[W(sACT, rstd_done)], inc=sDVE)

        def msq_f(e, tt=tt):
            e.tensor_scalar(stats[:, 0:1], stats[:, 1:2], LN_EPS, None, ALU.add)
            e.drain()
            e.reciprocal(stats[:, 3:4], stats[:, 0:1])
            e.drain()
            return e.tensor_tensor(stats[:, 0:1], stats[:, 1:2], stats[:, 3:4], ALU.mult)
        msq_done = step("vector", msq_f, waits=[W(sACT, rstd_done)], inc=sDVE)

        def rmsr_f(e, tt=tt):
            e.activation(stats[:, 3:4], stats[:, 0:1], AF.Ln, bias=consts[:, 4:5])
            e.drain()
            return e.activation(rmsr8[:, tt:tt + 1], stats[:, 3:4], AF.Exp,
                                scale=consts[:, 3:4])
        rmsr_done[tt] = step("scalar", rmsr_f, waits=[W(sDVE, msq_done)], inc=sACT)

        ln_st_done[tt] = dma("gpsimd", sG,
                             lambda tt=tt: xn_tm_d.ap()[tt * 128:(tt + 1) * 128, :],
                             lambda xnt=xnt: xnt[:, :],
                             waits=[W(sDVE, ln_xn_done[tt])])

        for g_ in range(G):
            bank = g_ % 2
            ptr = ptr_a if bank == 0 else ptr_b
            w_tr = [W(sDVE, ln_xn_done[tt])]
            if bank_ev.get(bank, 0) > 0:
                w_tr.append(W(sDVE, bank_ev[bank]))
            step("tensor",
                 lambda e, xnt=xnt, g_=g_, ptr=ptr: e.transpose(
                     ptr, xnt[:, g_ * 128:(g_ + 1) * 128], eye_sb[:, :]),
                 waits=w_tr, inc=sPE)
            tr_cnt = sPE.n
            bank_ev[bank] = step(
                "vector",
                lambda e, tt=tt, g_=g_, ptr=ptr: e.tensor_copy(
                    xn_cm[:, g_ * TPC + tt * 128: g_ * TPC + (tt + 1) * 128], ptr),
                waits=[W(sPE, tr_cnt)], inc=sDVE)
        ln_ev_done[tt] = sDVE.n

    ln_all_ev = sDVE.n
    ln_half_ev = ln_ev_done[NTT // 2 - 1]
    CC_XH = []
    for half in range(2):
        gst = dma("gpsimd", sG,
                  lambda half=half: agx_in[half].ap().rearrange("(g p) t -> p g t", p=128),
                  lambda half=half: xn_cm[:, :].rearrange("p (g t) -> p g t", g=G)
                      [:, :, half * 512:(half + 1) * 512],
                  waits=[W(sDVE, ln_half_ev if half == 0 else ln_all_ev)])
        sCC.n += 1

        def agx_f(e, half=half, gst=gst):
            e.wait_ge(sG.sem, gst)
            return e.collective_compute("AllGather", ALU.bypass,
                                        replica_groups=[list(range(NCO))],
                                        ins=[agx_in[half].ap()],
                                        outs=[agx_out[half].ap()])
        P.on("gpsimd", lambda e, agx_f=agx_f: agx_f(e).then_inc(sCC.sem, 1))
        CC_XH.append(sCC.n)
    CC_X = sCC.n
    G_xcm = dma("gpsimd", sG,
                lambda: xn_cm_d.ap().rearrange("(g p) t -> p g t", p=128),
                lambda: xn_cm[:, :].rearrange("p (g t) -> p g t", g=G),
                waits=[W(sDVE, ln_all_ev)])

    tab_dve = []
    for tcx in range(TC):
        w = [W(sDVE, tab_dve[tcx - 1][1])] if tcx else []
        iot_ld = dma("sync", sL2, lambda: iotc[:, :],
                     lambda tcx=tcx: iota_t_in.ap()[:, tcx * 512:(tcx + 1) * 512],
                     waits=w)
        w_mm = [W(sL2, iot_ld), W(sACT, A_invf)]
        if tcx:
            w_mm.append(W(sDVE, tab_dve[tcx - 1][1]))
        frq_pe = step("tensor",
                      lambda e: e.matmul(PB[0], invf[:, :], iotc[:, :],
                                         start=True, stop=True),
                      waits=w_mm, inc=sPE)

        def redu_f(e):
            def reduce(dst, src_ap, shift):
                if shift:
                    e.tensor_scalar(dst, src_ap, shift, None, ALU.add)
                    e.drain()
                    src_ap = dst
                e.tensor_scalar(prt[:, :], src_ap, float(1.0 / (2 * np.pi)), None, ALU.mult)
                e.drain()
                e.tensor_copy(kint[:, :], prt[:, :])
                e.drain()
                e.tensor_copy(rt1[:, :], kint[:, :])
                e.drain()
                e.tensor_tensor(prt[:, :], rt1[:, :], prt[:, :], ALU.is_gt)
                e.drain()
                e.tensor_tensor(rt1[:, :], rt1[:, :], prt[:, :], ALU.subtract)
                e.drain()
                e.scalar_tensor_tensor(dst, rt1[:, :], -float(2 * np.pi),
                                        src_ap, ALU.mult, ALU.add)
                e.drain()
                return e.tensor_scalar(dst, dst, float(2 * np.pi), 0.0, ALU.min, ALU.max)
            reduce(frq0[:, :], PB[0], None)
            e.drain()
            return reduce(frq1[:, :], PB[0], float(np.pi / 2))
        rr = step("vector", redu_f, waits=[W(sPE, frq_pe)], inc=sDVE)
        tab_dve.append((rr, rr))

        def sin_f(e, tcx=tcx):
            e.activation(sin_sb[:, tcx * 512:(tcx + 1) * 512], frq0[:, :], AF.Sin,
                         bias=consts[:, 6:7])
            return e.activation(cos_sb[:, tcx * 512:(tcx + 1) * 512], frq1[:, :], AF.Sin,
                                bias=consts[:, 6:7])
        sa = step("scalar", sin_f, waits=[W(sDVE, rr)], inc=sACT)
        tab_dve[tcx] = (rr, rr, sa)
    tab_act = sACT.n
    tab_done = step("vector",
                    lambda e: e.tensor_scalar(sin_sb[64:128, :], sin_sb[64:128, :],
                                              -1.0, None, ALU.mult),
                    waits=[W(sACT, tab_act)], inc=sDVE)

    WQK = (wq_sb, wk_sb)
    mov_ld = {}
    mov_free = {}
    rope_m1 = {}
    rope_out = {}
    qk_stores = {}
    for tcx in range(TC):
        for half in range(2):
            idx = tcx * 2 + half
            mv = mv0 if half == 0 else mv1
            ch = tcx % 2
            w = [W(sCC, CC_XH[ch])]
            if idx >= 2:
                w.append(W(sPE, mov_free[idx - 2]))
            r = tcx // 2
            mov_ld[idx] = dma(
                "sync", sL,
                lambda mv=mv: mv[:, :].rearrange("p (g t) -> p g t", g=16),
                lambda r=r, half=half, ch=ch: agx_out[ch].ap()
                    .rearrange("(r g p) t -> r p g t", g=G, p=128)[r]
                    [:, half * 16:(half + 1) * 16, :],
                waits=w)

        for half in range(2):
            idx = tcx * 2 + half
            mv = mv0 if half == 0 else mv1
            for gg in range(16):
                g_ = half * 16 + gg
                for j in range(8):
                    w_mm = []
                    if g_ == 0 and j == 0:
                        w_mm = [W(sL, mov_ld[idx]), W(sG, G_wqk)]
                        if tcx == 0:
                            w_mm += [W(sDVE, tab_done), W(sDVE, ln_all_ev)]
                    elif gg == 0 and j == 0:
                        w_mm = [W(sL, mov_ld[idx])]
                    if g_ == 0 and tcx > 0:
                        w_mm.append(W(sDVE, rope_m1[(tcx - 1, j)]))
                    wsb = WQK[j // 4]
                    jj = j % 4
                    last = (g_ == G - 1)
                    step("tensor",
                         lambda e, wsb=wsb, g_=g_, jj=jj, j=j, mv=mv, gg=gg, first=(g_ == 0), last=last:
                         e.matmul(PB[j],
                                  wsb[:, g_ * 512 + jj * 128: g_ * 512 + (jj + 1) * 128],
                                  mv[:, gg * 512:(gg + 1) * 512],
                                  start=first, stop=last),
                         waits=w_mm, inc=(sPE if (last or gg == 15) else None))
            mov_free[idx] = sPE.n
        qk_pe_last = sPE.n

        for j in range(8):
            qkb = qkb0 if j % 2 == 0 else qkb1
            w_r = [W(sPE, qk_pe_last - 8 + j + 1)]
            prev = qk_stores.get((tcx, j - 2)) or (qk_stores.get((tcx - 1, j + 6)) if tcx else None)
            if prev:
                w_r.append(W(sG, prev))

            def rope_f1(e, j=j, tcx=tcx):
                e.tensor_copy(prt[0:64, :], PB[j][64:128, :])
                e.tensor_copy(prt[64:128, :], PB[j][0:64, :])
                e.drain()
                return e.tensor_tensor(rt1[:, :], PB[j],
                                       cos_sb[:, tcx * 512:(tcx + 1) * 512], ALU.mult)
            rope_m1[(tcx, j)] = step("vector", rope_f1, waits=w_r, inc=sDVE)

            def rope_f2(e, j=j, tcx=tcx, qkb=qkb):
                e.tensor_tensor(frq0[:, :], prt[:, :],
                                sin_sb[:, tcx * 512:(tcx + 1) * 512], ALU.mult)
                e.drain()
                return e.tensor_tensor(qkb[:, :], frq0[:, :], rt1[:, :], ALU.subtract)
            rope_out[(tcx, j)] = step("vector", rope_f2, inc=sDVE)

            dest = q_dram if j < 4 else k_dram
            qk_stores[(tcx, j)] = dma(
                "gpsimd", sG,
                lambda dest=dest, j=j, tcx=tcx: dest.ap()[(j % 4) * 128:(j % 4 + 1) * 128,
                                                          tcx * 512:(tcx + 1) * 512],
                lambda qkb=qkb: qkb[:, :],
                waits=[W(sDVE, rope_out[(tcx, j)])])
    qk_all_pe = sPE.n
    rope_all = sDVE.n
    qk_all_st = sG.n

    G_wv = dma("gpsimd", sG,
               lambda: wv_sb[:, :].rearrange("p (g n) -> p g n", g=G),
               lambda: wv_in.ap().rearrange("(g p) n -> p g n", p=128),
               waits=[W(sPE, qk_all_pe)])

    sv_ld = {}
    v_pe = {}
    v_ev = {}
    for vt in range(VT):
        sv = sv0 if vt % 2 == 0 else sv1
        w = []
        if vt == 0:
            w = [W(sPE, qk_all_pe), W(sDVE, rope_all)]
        if vt >= 2:
            w.append(W(sPE, v_pe[vt - 2]))
        r, tt_ = vt // 8, vt % 8
        ch, tl = tt_ // 4, tt_ % 4
        sv_ld[vt] = dma(
            "sync", sL,
            lambda sv=sv: sv[:, :].rearrange("p (g t) -> p g t", g=G),
            lambda r=r, ch=ch, tl=tl: agx_out[ch].ap()
                .rearrange("(r g p) t -> r p g t", g=G, p=128)[r]
                [:, :, tl * 128:(tl + 1) * 128],
            waits=w)

        pb = PPAIR[0][:, 0:512] if vt % 2 == 0 else PPAIR[0][:, 512:1024]
        w_mm = [W(sL, sv_ld[vt]), W(sG, G_wv)]
        if vt == 0:
            w_mm += [W(sG, G_xcm), W(sDVE, rope_all)]
        if vt >= 2:
            w_mm.append(W(sDVE, v_ev[vt - 2]))

        def vmm_f(e, sv=sv, pb=pb):
            ins = None
            for g_ in range(G):
                ins = e.matmul(pb[:, :], sv[:, g_ * 128:(g_ + 1) * 128],
                               wv_sb[:, g_ * 512:(g_ + 1) * 512],
                               start=(g_ == 0), stop=(g_ == G - 1))
            return ins
        v_pe[vt] = step("tensor", vmm_f, waits=w_mm, inc=sPE)
        v_ev[vt] = step("vector",
                        lambda e, vt=vt, pb=pb: e.tensor_copy(
                            v_sb[:, vt * 512:(vt + 1) * 512], pb[:, :]),
                        waits=[W(sPE, v_pe[vt])], inc=sDVE)
    v_all_pe = sPE.n
    v_all = sDVE.n

    G_wbr = dma("gpsimd", sG,
                lambda: wbr_sb[:, :].rearrange("p (g n) -> p g n", g=8),
                lambda: wbr_in.ap().rearrange("(g p) n -> p g n", p=128),
                waits=[W(sPE, qk_all_pe)])
    G_wall = dma("gpsimd", sG2, lambda: wall_sb[:, :], lambda: wall_in.ap())
    G_ffnw = sG.n

    d_entry_w = [W(sPE, v_all_pe), W(sDVE, rope_all),
                 W(sG, G_ffnw), W(sG2, G_wall)]
    wallT_ev = 0
    for g_ in range(G):
        ptr = ptr_c if g_ % 2 == 0 else ptr_c2
        w_tr = list(d_entry_w) if g_ == 0 else []
        if g_ >= 2:
            w_tr.append(W(sDVE, wallT_ev - 1))
        step("tensor",
             lambda e, g_=g_, ptr=ptr: e.transpose(
                 ptr[:, 0:24], wall_sb[0:24, g_ * 128:(g_ + 1) * 128], eye_sb[0:24, 0:24]),
             waits=w_tr, inc=sPE)
        trc = sPE.n
        w_ev = [W(sPE, trc)]
        wallT_ev = step("vector",
                        lambda e, g_=g_, ptr=ptr: e.tensor_copy(
                            wallT[:, g_ * 24:(g_ + 1) * 24], ptr[:, 0:24]),
                        waits=w_ev, inc=sDVE)

    bb_pe = step("tensor",
                 lambda e: e.matmul(PPAIR[2][:, 0:24], onesf[:, :], basesr[:, :],
                                    start=True, stop=True),
                 waits=[W(sL2, L2_const), W(sDVE, wallT_ev)], inc=sPE)
    bb_ev = step("vector", lambda e: e.tensor_copy(basesb[:, :], PPAIR[2][:, 0:24]),
                 waits=[W(sPE, bb_pe)], inc=sDVE)

    gact = step("scalar",
                lambda e: e.activation(gatesb[0:1, 1:2], gatesb[0:1, 0:1], AF.Exp,
                                       scale=consts[0:1, 5:6]),
                waits=[W(sL2, L2_const), W(sPL, P_const)], inc=sACT)

    def gate_f(e):
        e.tensor_scalar(gatesb[0:1, 2:3], gatesb[0:1, 1:2], 1.0, None, ALU.add)
        e.drain()
        e.reciprocal(gatesb[0:1, 3:4], gatesb[0:1, 2:3])
        e.drain()
        e.tensor_copy(gcat[0:1, 0:1], gatesb[0:1, 3:4])
        return e.tensor_tensor(gcat[0:1, 1:2], gatesb[0:1, 1:2], gatesb[0:1, 3:4],
                               ALU.mult)
    g_dve = step("vector", gate_f, waits=[W(sACT, gact)], inc=sDVE)
    gb_pe = step("tensor",
                 lambda e: e.matmul(PPAIR[2][:, 32:34], onesf[:, :], gcat[:, :],
                                    start=True, stop=True),
                 waits=[W(sDVE, g_dve), W(sDVE, bb_ev)], inc=sPE)
    gb_ev = step("vector", lambda e: e.tensor_copy(gb[:, :], PPAIR[2][:, 32:34]),
                 waits=[W(sPE, gb_pe)], inc=sDVE)
    def i16_f(e):
        e.memset(i16b[:, :], 0.0)
        e.drain()
        ins = None
        for pos in (0, 5, 10, 15):
            ins = e.tensor_copy(
                i16b[:, :].rearrange("p (t j) -> p t j", j=16)[:, :, pos],
                gb[:, 1:2].broadcast_to((128, 8)))
        return ins
    i16_done = step("vector", i16_f, waits=[W(sDVE, gb_ev)], inc=sDVE)

    cur_stage[0] = 'dlog'
    sv2_free = {}
    lg_ev = 0
    for tt in range(NTT):
        svb = sv2 if tt % 2 == 0 else sv3
        w = [W(sG, G_xcm)]
        if tt < 2:
            w.append(W(sPE, qk_all_pe))
        if sv2_free.get(tt - 2):
            w.append(W(sPE, sv2_free[tt - 2]))
        ld_sv = dma("sync", sL,
                    lambda svb=svb: svb[:, :].rearrange("p (g t) -> p g t", g=G),
                    lambda tt=tt: xn_cm_d.ap()
                        .rearrange("(g p) t -> p g t", p=128)[:, :, tt * 128:(tt + 1) * 128],
                    waits=w)

        def lg_mm(e, svb=svb):
            ins = None
            for g_ in range(G):
                ins = e.matmul(PPAIR[2][:, 0:24], svb[:, g_ * 128:(g_ + 1) * 128],
                               wallT[:, g_ * 24:(g_ + 1) * 24],
                               start=(g_ == 0), stop=(g_ == G - 1))
            return ins
        w_lg = [W(sL, ld_sv), W(sDVE, wallT_ev), W(sDVE, gb_ev)]
        if lg_ev > 0:
            w_lg.append(W(sDVE, lg_ev))
        sv2_free[tt] = step("tensor", lg_mm, waits=w_lg, inc=sPE)
        if "dbgf" in _TAPS and tt == 0:
            dma("gpsimd", sG, lambda svb=svb: dbgf.ap()[:, 4096:8192],
                lambda svb2=svb: svb2[:, :], waits=list(w_lg))

        def lg_ev_f(e, tt=tt):
            e.tensor_scalar(logits[:, tt * 24:(tt + 1) * 24], PPAIR[2][:, 0:24],
                            rmsr8[:, tt:tt + 1], None, ALU.mult)
            e.drain()
            return e.tensor_tensor(logits[:, tt * 24:(tt + 1) * 24],
                                   logits[:, tt * 24:(tt + 1) * 24], basesb[:, :], ALU.add)
        lg_ev = step("vector", lg_ev_f,
                     waits=[W(sPE, sv2_free[tt]), W(sACT, rmsr_done[NTT - 1]),
                            W(sDVE, bb_ev)], inc=sDVE)
    cur_stage[0] = 'dsink'
    lg_all = sDVE.n

    ffn_st = {}

    def d_tail_gen():
        lg4 = logits[:, :].rearrange("p (t n) -> p t n", n=24)
        zv4 = zv[:, :].rearrange("p (t i j) -> p t i j", i=4, j=4)
        s14 = s1b[:, :].rearrange("p (t i j) -> p t i j", i=4, j=4)
        s24 = s2b[:, :].rearrange("p (t i j) -> p t i j", i=4, j=4)
        e14 = e1b[:, :].rearrange("p (t i j) -> p t i j", i=4, j=4)
        su3 = sub_[:, :].rearrange("p (t i) -> p t i", i=4)
        sv3 = svb_[:, :].rearrange("p (t j) -> p t j", j=4)
        ul3 = ulb[:, :].rearrange("p (t i) -> p t i", i=4)
        vl3 = vlb[:, :].rearrange("p (t j) -> p t j", j=4)

        def sink_init(e):
            e.tensor_scalar(zv[:, :].rearrange("p (t i j) -> p t i j", i=4, j=4),
                            lg4[:, :, 0:16].rearrange("p t (i j) -> p t i j", j=4),
                            10.0, None, ALU.mult)
            return e.memset(vlb[:, :], 0.0)
        sk = step("vector", sink_init, waits=[W(sDVE, lg_all)], inc=sDVE)

        for itr in range(SINK_ITERS):
            yield
            sk = step("vector",
                      lambda e: e.tensor_tensor(s14, zv4, vl3.unsqueeze(2).broadcast_to(
                          (128, 8, 4, 4)), ALU.subtract),
                      waits=[W(sDVE, sk)] if itr == 0 else [W(sACT, sk)], inc=sDVE)
            sk = step("scalar", lambda e: e.activation(e1b[:, :], s1b[:, :], AF.Exp),
                      waits=[W(sDVE, sk)], inc=sACT)
            sk = step("vector", lambda e: e.tensor_reduce(su3, e14, AX.X, ALU.add),
                      waits=[W(sACT, sk)], inc=sDVE)
            sk = step("scalar", lambda e: e.activation(ulb[:, :], sub_[:, :], AF.Ln),
                      waits=[W(sDVE, sk)], inc=sACT)
            sk = step("vector",
                      lambda e: e.tensor_tensor(s24, zv4, ul3.unsqueeze(3).broadcast_to(
                          (128, 8, 4, 4)), ALU.subtract),
                      waits=[W(sACT, sk)], inc=sDVE)
            sk = step("scalar", lambda e: e.activation(e1b[:, :], s2b[:, :], AF.Exp),
                      waits=[W(sDVE, sk)], inc=sACT)

            def vsum_f(e):
                e.tensor_tensor(sv3, e14[:, :, 0, :], e14[:, :, 1, :], ALU.add)
                e.drain()
                e.tensor_tensor(sv3, sv3, e14[:, :, 2, :], ALU.add)
                e.drain()
                return e.tensor_tensor(sv3, sv3, e14[:, :, 3, :], ALU.add)
            sk = step("vector", vsum_f, waits=[W(sACT, sk)], inc=sDVE)
            sk = step("scalar", lambda e: e.activation(vlb[:, :], svb_[:, :], AF.Ln),
                      waits=[W(sDVE, sk)], inc=sACT)

        sk = step("vector",
                  lambda e: e.tensor_tensor(s14, s24, vl3.unsqueeze(2).broadcast_to(
                      (128, 8, 4, 4)), ALU.subtract),
                  waits=[W(sACT, sk)], inc=sDVE)
        sk = step("scalar", lambda e: e.activation(hres[:, :], s1b[:, :], AF.Exp),
                  waits=[W(sDVE, sk)], inc=sACT)
        hres_done = step("vector",
                         lambda e: e.scalar_tensor_tensor(hres[:, :], hres[:, :],
                                                          gb[:, 0:1], i16b[:, :],
                                                          ALU.mult, ALU.add),
                         waits=[W(sACT, sk), W(sDVE, i16_done)], inc=sDVE)

        hh = {}
        for name, off, buf in (("pre", 16, hpre), ("post", 20, hpost)):
            w_ea = [W(sDVE, lg_all)]
            if "dbgf" in _TAPS:
                w_ea.append(W(sG, sG.n))
            if name == "post":
                w_ea.append(W(sDVE, hh["pre"]))
            ea = step("scalar",
                      lambda e, off=off: e.activation(
                          hwork[:, :].rearrange("p (t j) -> p t j", j=4),
                          lg4[:, :, off:off + 4], AF.Exp),
                      waits=w_ea, inc=sACT)

            def sm_f(e, buf=buf):
                e.tensor_reduce(rsum8[:, :], hwork[:, :].rearrange("p (t j) -> p t j", j=4),
                                AX.X, ALU.add)
                e.drain()
                e.reciprocal(rcp8[:, :], rsum8[:, :])
                e.drain()
                return e.tensor_tensor(buf[:, :].rearrange("p (t j) -> p t j", j=4),
                                       hwork[:, :].rearrange("p (t j) -> p t j", j=4),
                                       rcp8[:, :].rearrange("p t -> p t").unsqueeze(2)
                                       .broadcast_to((128, 8, 4)), ALU.mult)
            hh[name] = step("vector", sm_f, waits=[W(sACT, ea)], inc=sDVE)
        h_all = sDVE.n
        if "dbgf" in _TAPS:
            dma("gpsimd", sG, lambda: dbgf.ap()[:, 0:192], lambda: logits[:, :],
                waits=[W(sDVE, h_all)])
            dma("gpsimd", sG, lambda: dbgf.ap()[:, 192:320], lambda: hres[:, :],
                waits=[W(sDVE, h_all)])
            dma("gpsimd", sG, lambda: dbgf.ap()[:, 320:352], lambda: hpre[:, :],
                waits=[W(sDVE, h_all)])
            dma("gpsimd", sG, lambda: dbgf.ap()[:, 352:384], lambda: hpost[:, :],
                waits=[W(sDVE, h_all)])
            dma("gpsimd", sG, lambda: dbgf.ap()[:, 384:392], lambda: rsum8[:, :],
                waits=[W(sDVE, lg_all)])
            dma("gpsimd", sG, lambda: dbgf.ap()[:, 392:400], lambda: rcp8[:, :],
                waits=[W(sDVE, lg_all)])
            dma("gpsimd", sG, lambda: dbgf.ap()[:, 504:512], lambda: rmsr8[:, :],
                waits=[W(sDVE, lg_all)])

        cur_stage[0] = 'dapp'
        xnm_free2 = {}
        y_free = {0: 0, 1: 0}
        ffnt_free = 0
        xpre_tr_ev = {0: 0, 1: 0}
        for tt in range(NTT):
            yield
            yield
            xnm = xnm0 if tt % 2 == 0 else xnm1
            w_x = [W(sDVE, lg_all), W(sPE, sv2_free[NTT - 1])]
            if xnm_free2.get(tt - 2):
                w_x.append(W(sDVE, xnm_free2[tt - 2]))
            ld_xn = dma("sync", sL, lambda xnm=xnm: xnm[:, :],
                        lambda tt=tt: xn_tm_d.ap()[tt * 128:(tt + 1) * 128, :],
                        waits=w_x)

            def xpre_fn(e, xnm=xnm, tt=tt):
                e.tensor_scalar(xpre_f[:, :], xnm[:, 0:1024], hpre[:, tt * 4:tt * 4 + 1],
                                None, ALU.mult)
                for n in (1, 2, 3):
                    e.drain()
                    e.scalar_tensor_tensor(xpre_f[:, :], xnm[:, n * 1024:(n + 1) * 1024],
                                           hpre[:, tt * 4 + n:tt * 4 + n + 1], xpre_f[:, :],
                                           ALU.mult, ALU.add)
                e.drain()
                return e.tensor_copy(xpre_b[:, :], xpre_f[:, :])
            w_xp = [W(sL, ld_xn), W(sDVE, h_all), W(sDVE, hres_done)]
            xp = step("vector", xpre_fn, waits=w_xp, inc=sDVE)

            for j in range(8):
                ptr = ptr_c if j % 2 == 0 else ptr_c2
                w_tr = [W(sDVE, xp)]
                if xpre_tr_ev[j % 2] > 0:
                    w_tr.append(W(sDVE, xpre_tr_ev[j % 2]))
                if j == 0 and y_free[1] > 0:
                    w_tr.append(W(sDVE, y_free[1]))
                step("tensor",
                     lambda e, j=j, ptr=ptr: e.transpose(
                         ptr, xpre_b[:, j * 128:(j + 1) * 128], eye_sb[:, :]),
                     waits=w_tr, inc=sPE)
                trc = sPE.n
                xpre_tr_ev[j % 2] = step(
                    "vector",
                    lambda e, j=j, ptr=ptr: e.tensor_copy(
                        xpre_cm[:, j * 128:(j + 1) * 128], ptr),
                    waits=[W(sPE, trc)], inc=sDVE)
            xpc = sDVE.n

            y_pe = {}
            for cb in range(2):
                w_y = [W(sDVE, xpc), W(sG, G_wbr)]
                if y_free[cb] > 0:
                    w_y.append(W(sDVE, y_free[cb]))

                def ymm(e, cb=cb):
                    ins = None
                    for ct in range(8):
                        ins = e.matmul(PPAIR[2][:, cb * 512:cb * 512 + 512], xpre_cm[:, ct * 128:(ct + 1) * 128],
                                       wbr_sb[:, ct * 1024 + cb * 512: ct * 1024 + cb * 512 + 512],
                                       start=(ct == 0), stop=(ct == 7))
                    return ins
                y_pe[cb] = step("tensor", ymm, waits=w_y, inc=sPE)

            last_comb = 0
            for n in range(4):
                for cb in range(2):
                    def comb_f(e, n=n, cb=cb, tt=tt, xnm=xnm):
                        sl = slice(cb * 512, cb * 512 + 512)
                        e.tensor_scalar(xpre_f[:, sl], PPAIR[2][:, cb * 512:cb * 512 + 512],
                                        hpost[:, tt * 4 + n:tt * 4 + n + 1], None, ALU.mult)
                        ins = None
                        for j in range(4):
                            e.drain()
                            dst = ffnt[:, n * 1024 + cb * 512: n * 1024 + cb * 512 + 512] \
                                if j == 3 else xpre_f[:, sl]
                            ins = e.scalar_tensor_tensor(
                                dst, xnm[:, j * 1024 + cb * 512: j * 1024 + cb * 512 + 512],
                                hres[:, tt * 16 + n * 4 + j: tt * 16 + n * 4 + j + 1],
                                xpre_f[:, sl], ALU.mult, ALU.add)
                        return ins
                    w_c = [W(sPE, y_pe[cb])]
                    if n == 0 and cb == 0 and ffnt_free > 0:
                        w_c.append(W(sG, ffnt_free))
                    last_comb = step("vector", comb_f, waits=w_c, inc=sDVE)
            y_free[0] = last_comb
            y_free[1] = last_comb
            xnm_free2[tt] = last_comb
            ffn_st[tt] = dma("gpsimd", sG,
                             lambda tt=tt: ffn_d.ap()[tt * 128:(tt + 1) * 128, :],
                             lambda: ffnt[:, :],
                             waits=[W(sDVE, last_comb)])
            ffnt_free = ffn_st[tt]
        d_pe_last = sPE.n
        d_dve_last = sDVE.n
        ffn_all_st = sG.n

        cur_stage[0] = None

        while True:
            yield
    d_gen = d_tail_gen()
    kh_ld = {}
    L_cmask = dma("sync", sL, lambda: cmask_sb[:, :], lambda: cmask_in.ap(),
                  waits=[W(sPE, v_all_pe), W(sDVE, rope_all)])
    kh_ld[0] = dma("sync", sL, lambda: kh0[:, :], lambda: k_dram.ap()[0:128, :],
                   waits=[W(sG, qk_all_st)])

    exp_on_bank = {}
    pv_on_p = {}
    qt_ld = {}
    qt_free = {}
    kh_free = {}
    at_store = {}
    at_cons = 0
    po_free = 0
    rec_free = 0
    c_entry = 0
    it = 0
    for h in range(NH):
        kh = kh0 if h % 2 == 0 else kh1
        if h + 1 < NH:
            khn = kh1 if (h + 1) % 2 == 1 else kh0
            w = [W(sPE, kh_free[h - 1])] if h >= 1 else [W(sPE, v_all_pe), W(sDVE, rope_all)]
            kh_ld[h + 1] = dma("sync", sL, lambda khn=khn: khn[:, :],
                               lambda h=h: k_dram.ap()[(h + 1) * 128:(h + 2) * 128, :],
                               waits=w)
        for qb in range(QB):
            qt = qt0 if it % 2 == 0 else qt1
            at = at0 if it % 2 == 0 else at1
            nkc = 4 * (qb + 1)
            w = [W(sG, qk_all_st)] if it == 0 else []
            if it >= 2:
                w.append(W(sPE, qt_free[it - 2]))
            qt_ld[it] = dma("sync", sL, lambda qt=qt: qt[:, :],
                            lambda h=h, qb=qb: q_dram.ap()[h * 128:(h + 1) * 128,
                                                           qb * 512:(qb + 1) * 512],
                            waits=w)
            mm1_last = 0
            exp_cnt = {}
            p_rdy = {}
            npairs = nkc // 2

            def emit_pv_pair(kp):
                for half in (0, 1):
                    kc = 2 * kp + half
                    pbs = pb3[kp % 3][:, half * 512:half * 512 + 512]
                    eng, val = p_rdy[kc]
                    w2 = [(eng, val)]
                    if kc == 0 and po_free > 0:
                        w2.append(W(sDVE, po_free))
                    step("tensor",
                         lambda e, kc=kc, h=h, pbs=pbs, first=(kc == 0), last=(kc == nkc - 1):
                         e.matmul(PO[:, :],
                                  v_sb[:, kc * 512 + h * 128: kc * 512 + (h + 1) * 128],
                                  pbs, start=first, stop=last),
                         waits=w2)
                    w3 = []
                    if kc == 0 and rec_free > 0:
                        w3 = [W(sACT, rec_free)]
                    cnt = step(
                        "tensor",
                        lambda e, pbs=pbs, first=(kc == 0), last=(kc == nkc - 1):
                        e.matmul(PR[:, :], ones_sb[:, :], pbs,
                                 start=first, stop=last),
                        waits=w3, inc=(sPE if half == 1 else None))
                    if half == 1:
                        pv_on_p[kp % 3] = cnt

            for kp in range(npairs):
                rb = kp % 2
                SP = PPAIR[rb]
                PPB = pb3[kp % 3]
                w_mm1 = []
                if kp == 0:
                    w_mm1 = [W(sL, qt_ld[it]), W(sL, kh_ld[h])]
                    if it == 0:
                        w_mm1 += [W(sDVE, v_all), W(sDVE, rope_all)]
                if exp_on_bank.get(rb, 0) > 0:
                    w_mm1.append(W(sACT, exp_on_bank[rb]))
                step("tensor",
                     lambda e, SP=SP, kh=kh, kc=2 * kp, qt=qt:
                     e.matmul(SP[:, 0:512], kh[:, kc * 128:(kc + 1) * 128],
                              qt[:, :], start=True, stop=True),
                     waits=w_mm1)
                mm1 = step("tensor",
                           lambda e, SP=SP, kh=kh, kc=2 * kp + 1, qt=qt:
                           e.matmul(SP[:, 512:1024], kh[:, kc * 128:(kc + 1) * 128],
                                    qt[:, :], start=True, stop=True),
                           waits=(), inc=sPE)
                mm1_last = mm1
                if kp >= 2:
                    emit_pv_pair(kp - 2)
                w_exp = [W(sPE, mm1)]
                if pv_on_p.get(kp % 3, 0) > 0:
                    w_exp.append(W(sPE, pv_on_p[kp % 3]))
                exp_on_bank[rb] = step(
                    "scalar",
                    lambda e, PPB=PPB, SP=SP: e.activation(
                        PPB[:, :], SP[:, :], AF.Exp,
                        bias=consts[:, 0:1], scale=consts[:, 1:2]),
                    waits=w_exp, inc=sACT)
                for half in (0, 1):
                    kc = 2 * kp + half
                    if kc >= nkc - 4:
                        i_ = kc - (nkc - 4)
                        moff = 384 - 128 * i_
                        p_rdy[kc] = (sDVE.sem, step(
                            "vector",
                            lambda e, PPB=PPB, half=half, moff=moff: e.tensor_tensor(
                                PPB[:, half * 512:half * 512 + 512],
                                PPB[:, half * 512:half * 512 + 512],
                                cmask_sb[:, moff:moff + 512], ALU.mult),
                            waits=[W(sACT, exp_on_bank[rb]), W(sL, L_cmask)], inc=sDVE))
                    else:
                        p_rdy[kc] = (sACT.sem, exp_on_bank[rb])
            emit_pv_pair(npairs - 2)
            emit_pv_pair(npairs - 1)
            qt_free[it] = mm1_last
            pv_last = sPE.n

            def recf(e):
                e.activation(rec0[:, :], PR[:, :], AF.Ln)
                e.drain()
                return e.activation(rec1[:, :], rec0[:, :], AF.Exp, scale=consts[:, 5:6])
            w_rec = [W(sPE, pv_last)]
            if at_cons > 0:
                w_rec.append(W(sDVE, at_cons))
            rec_free = step("scalar", recf, waits=w_rec, inc=sACT)

            w_at = [W(sACT, rec_free), W(sPE, pv_last)]
            if at_store.get(it - 2):
                w_at.append(W(sG, at_store[it - 2]))
            at_cons = step("vector",
                           lambda e, at=at: e.tensor_tensor(at[:, :], PO[:, :],
                                                            rec1[:, :], ALU.mult),
                           waits=w_at, inc=sDVE)
            po_free = at_cons
            at_store[it] = dma("gpsimd", sG,
                               lambda h=h, qb=qb: aga_in.ap()[h * 128:(h + 1) * 128,
                                                              qb * 512:(qb + 1) * 512],
                               lambda at=at: at[:, :],
                               waits=[W(sDVE, at_cons)])
            it += 1
            next(d_gen)
            if qb % 2 == 0:
                next(d_gen)
        kh_free[h] = qt_free[it - 1]
        h_store = sG.n
        sCC.n += 1
        def agh_f(e, h=h, h_store=h_store):
            e.wait_ge(sG.sem, h_store)
            return e.collective_compute("AllGather", ALU.bypass,
                                        replica_groups=[list(range(NCO))],
                                        ins=[aga_in.ap()[h * 128:(h + 1) * 128, :]],
                                        outs=[agh_out[h].ap()])
        P.on("gpsimd", lambda e, h=h, h_store=h_store: agh_f(e, h, h_store).then_inc(sCC.sem, 1))
    for _ in range(120):
        next(d_gen)
    c_pe_last = sPE.n
    c_dve_last = sDVE.n
    c_act_last = sACT.n
    at_all = sG.n
    d_pe_last = c_pe_last
    d_dve_last = c_dve_last

    CC_A = sCC.n

    def asb_ld_f(e):
        pid = nc.partition_id(engines=[mybir.EngineType.SP])
        off = e.compute_val(pid * TPC)
        for hh in range(NH):
            e.wait_ge(sCC.sem, CC_X + 1 + hh)
            if hh == 0:
                e.wait_ge(sPE.sem, c_pe_last)
            e.dma_start(
                out=a_sb[:, :].rearrange("p (r hh t) -> p r hh t", hh=NH, t=TPC)[:, :, hh, :],
                in_=agh_out[hh].ap().rearrange("(r p) t -> p r t", p=128)[:, :, bass.ds(off, TPC)]
            ).then_inc(sL.sem, 16)
    sL.n += 64
    P.on("sync", asb_ld_f)
    L_asb = sL.n

    wo_ld = {}
    wo_free = {}
    for mb in range(2):
        wob = wo_b0 if mb % 2 == 0 else wo_b1
        wo_ld[mb] = dma("gpsimd", sG,
                        lambda wob=wob: wob[:, :].rearrange("p (g n) -> p g n", g=G),
                        lambda mb=mb: wo_in.ap()[:, mb * 512:(mb + 1) * 512]
                            .rearrange("(g p) n -> p g n", p=128),
                        waits=[W(sPE, d_pe_last), W(sDVE, d_dve_last)])

    os_free = {0: 0, 1: 0}
    st_out = {}
    e_mm_done = {}
    for mb in range(8):
        wob = wo_b0 if mb % 2 == 0 else wo_b1
        if mb >= 2:
            wo_ld[mb] = dma("gpsimd", sG,
                            lambda wob=wob: wob[:, :].rearrange("p (g n) -> p g n", g=G),
                            lambda mb=mb: wo_in.ap()[:, mb * 512:(mb + 1) * 512]
                                .rearrange("(g p) n -> p g n", p=128),
                            waits=[W(sPE, e_mm_done[(mb - 2, NTT - 1)])])
        for tt in range(NTT):
            k = mb * NTT + tt
            bank = 6 + k % 2
            xsb = xs0 if k % 2 == 0 else xs1
            fsb = fs0 if k % 2 == 0 else fs1
            osb = os0 if k % 2 == 0 else os1
            w_xs = [W(sPE, c_pe_last)]
            if k >= 2:
                w_xs.append(W(sDVE, os_free[k % 2]))
            ld_xs = dma("sync", sL, lambda xsb=xsb: xsb[:, :],
                        lambda tt=tt, mb=mb: x_in.ap()[tt * 128:(tt + 1) * 128,
                                                       mb * 512:(mb + 1) * 512],
                        waits=w_xs)
            ld_fs = dma("sync", sL, lambda fsb=fsb: fsb[:, :],
                        lambda tt=tt, mb=mb: ffn_d.ap()[tt * 128:(tt + 1) * 128,
                                                        mb * 512:(mb + 1) * 512],
                        waits=[W(sG, ffn_st[tt])])

            def emm(e, tt=tt, bank=bank, wob=wob):
                ins = None
                for g_ in range(G):
                    ins = e.matmul(PB[bank],
                                   a_sb[:, g_ * TPC + tt * 128: g_ * TPC + (tt + 1) * 128],
                                   wob[:, g_ * 512:(g_ + 1) * 512],
                                   start=(g_ == 0), stop=(g_ == G - 1))
                return ins
            w_mm = [W(sL, L_asb), W(sG, wo_ld[mb])]
            if k == 0:
                w_mm += [W(sDVE, c_dve_last), W(sACT, c_act_last)]
            if k >= 2:
                w_mm.append(W(sDVE, os_free[k % 2]))
            e_mm_done[(mb, tt)] = step("tensor", emm, waits=w_mm, inc=sPE)

            def add_f(e, osb=osb, xsb=xsb, fsb=fsb, bank=bank):
                e.tensor_tensor(osb[:, :], PB[bank], xsb[:, :], ALU.add)
                e.drain()
                return e.tensor_tensor(osb[:, :], osb[:, :], fsb[:, :], ALU.add)
            w_add = [W(sPE, e_mm_done[(mb, tt)]), W(sL, ld_xs), W(sL, ld_fs)]
            if k >= 2 and (k - 2) in st_out:
                w_add.append(W(sL, st_out[k - 2]))
            os_free[k % 2] = step("vector", add_f, waits=w_add, inc=sDVE)
            st_out[k] = dma("sync", sL,
                            lambda tt=tt, mb=mb: out_t.ap()[tt * 128:(tt + 1) * 128,
                                                            mb * 512:(mb + 1) * 512],
                            lambda osb=osb: osb[:, :],
                            waits=[W(sDVE, os_free[k % 2])])

    for (tout, srct) in tap_outs:
        dma("gpsimd", sG2, lambda tout=tout: tout.ap(), lambda srct=srct: srct.ap(),
            waits=[W(sL, sL.n), W(sG, sG.n), W(sDVE, sDVE.n), W(sACT, sACT.n)])

    fin_L, fin_G, fin_G2 = sL.n, sG.n, sG2.n
    P.on("sync", lambda e: e.wait_ge(sL.sem, fin_L))
    P.on("gpsimd", lambda e: e.wait_ge(sG.sem, fin_G))
    P.on("gpsimd", lambda e: e.wait_ge(sG2.sem, fin_G2))

    with nc.Block() as blk:
        P.flush(blk)
    es.close()
    return nc


_cache = {}


def kernel(**inputs):
    bf = ml_dtypes.bfloat16
    x = np.asarray(inputs["L0_attn_hidden_states"])
    wq = np.asarray(inputs["L0_attn_wq"])
    wk = np.asarray(inputs["L0_attn_wk"])
    wv = np.asarray(inputs["L0_attn_wv"])
    wo = np.ascontiguousarray(np.asarray(inputs["L0_attn_wo"]))
    wbr = np.ascontiguousarray(np.asarray(inputs["L0_ffn_W_branch"]))
    wall = np.ascontiguousarray(np.concatenate([
        np.asarray(inputs["L0_ffn_W_res"]),
        np.asarray(inputs["L0_ffn_W_pre"]),
        np.asarray(inputs["L0_ffn_W_post"])], axis=0))
    bases = np.ascontiguousarray(np.concatenate([
        np.asarray(inputs["L0_ffn_H_res_base"]).reshape(-1),
        np.asarray(inputs["L0_ffn_H_pre_base"]).reshape(-1),
        np.asarray(inputs["L0_ffn_H_post_base"]).reshape(-1)])[None, :]).astype(np.float32)
    gate = np.ascontiguousarray(np.asarray(inputs["L0_ffn_gate"]).reshape(1, 1)).astype(np.float32)

    eye = np.eye(128, dtype=np.float32).astype(bf)
    ones = np.ones((128, 128), dtype=np.float32).astype(bf)
    iota_d = (np.arange(128, dtype=np.int64) % 64).astype(np.float32)[None, :]
    iota_t = np.arange(S, dtype=np.int64).astype(np.float32)[None, :]
    k_idx = np.arange(128)[:, None]
    j_idx = np.arange(1024)[None, :]
    cmask = np.ascontiguousarray(((j_idx - 384) >= k_idx).astype(np.float32).astype(bf))

    if "nc" not in _cache:
        _cache["nc"] = build_program()
    nc = _cache["nc"]

    in_maps = []
    for c in range(NCO):
        in_maps.append({
            "x_slice": np.ascontiguousarray(x[0, c * TPC:(c + 1) * TPC, :]),
            "wq_s": np.ascontiguousarray(wq[:, c * 512:(c + 1) * 512]),
            "wk_s": np.ascontiguousarray(wk[:, c * 512:(c + 1) * 512]),
            "wv_s": np.ascontiguousarray(wv[:, c * 512:(c + 1) * 512]),
            "wo": wo, "wbr": wbr, "wall": wall, "bases": bases, "gate": gate,
            "eye": eye, "onesb": ones, "iota_d": iota_d, "iota_t": iota_t,
            "cmask": cmask,
        })
    res = run_bass_kernel_spmd(nc, in_maps, core_ids=list(range(NCO)))
    if _TAPS:
        _cache["taps"] = res
    out = np.concatenate([res.results[c]["out"] for c in range(NCO)], axis=0)
    return out.reshape(1, S, D)
